# version 03 (final)
# baseline (speedup 1.0000x reference)
import math
import numpy as np
import concourse.bass as bass
import concourse.mybir as mybir
from concourse.bass_utils import run_bass_kernel_spmd
import ml_dtypes

F32 = mybir.dt.float32
BF16 = mybir.dt.bfloat16
F16 = mybir.dt.float16
I8 = mybir.dt.int8
U32 = mybir.dt.uint32
AF = mybir.ActivationFunctionType
ALU = mybir.AluOpType
AX = mybir.AxisListType

D = 2048
S = 8192
H = 16
DH = 128
NSTRIPE = 16
STRIPE = 512
NCORES = 8
RSQ = float(1.0 / np.float32(np.sqrt(np.float32(128.0))))
MAGIC = 3 * 2.0 ** 22
BF16NP = ml_dtypes.bfloat16


def stripes_of(core):
    return sorted([core, NSTRIPE - 1 - core])



ENGINES = ("sync", "tensor", "vector", "scalar", "gpsimd")


class Chan:
    def __init__(self, name, step):
        self.name = name
        self.step = step
        self.count = 0
        self.handle = None

    def push(self):
        self.count += self.step
        return (self, self.count)

    def tok(self):
        return (self, self.count)


class Prog:
    def __init__(self):
        self.ops = {e: [] for e in ENGINES}

    def emit(self, eng, method, *args, wait=None, inc=None, **kwargs):
        for t in (wait or []) + (inc or []):
            assert isinstance(t, tuple) and len(t) == 2 and isinstance(t[1], int), \
                f"bad token {t!r} in {method} {eng}"
        self.ops[eng].append((method, args, kwargs, wait or [], inc or []))

    def replay(self, eng_name, eng):
        for method, args, kwargs, waits, incs in self.ops[eng_name]:
            for (ch, val) in waits:
                if val > 0:
                    eng.wait_ge(ch.handle, val)
            if method is None:
                inst = None
            else:
                inst = getattr(eng, method)(*args, **kwargs)
            for (ch, val) in incs:
                inst = inst.then_inc(ch.handle, ch.step)

    def dma(self, eng, out, in_, wait=None, inc=None, **kw):
        self.emit(eng, "dma_start", out=out, in_=in_, wait=wait, inc=inc, **kw)

    def barrier(self, eng, tokens):
        self.emit(eng, None, wait=tokens)


class Builder:
    def __init__(self):
        self.nc = bass.Bass("TRN2", target_bir_lowering=False)
        self.prog = Prog()
        self.chans = []
        self.consts = {}
        self._uid = 0

    def uid(self):
        self._uid += 1
        return self._uid

    def chan(self, name, step=1):
        c = Chan(f"{name}{self.uid()}", step)
        self.chans.append(c)
        return c

    def sbuf(self, name, shape, dtype):
        return self.nc.alloc_sbuf_tensor(f"{name}_{self.uid()}", shape, dtype)

    def dram(self, name, shape, dtype, kind="Internal", **kw):
        return self.nc.dram_tensor(name, shape, dtype, kind=kind, **kw)

    def const(self, name, arr):
        if name not in self.consts:
            self.consts[name] = self.nc.inline_tensor(np.ascontiguousarray(arr), name=name)
        return self.consts[name]

    def finish(self):
        nc = self.nc
        import contextlib
        with contextlib.ExitStack() as stack:
            for c in self.chans:
                c.handle = stack.enter_context(nc.semaphore(c.name))
            block = stack.enter_context(nc.Block())
            for e in ENGINES:
                getattr(block, e)(lambda eng, _e=e: self.prog.replay(_e, eng))
        return nc


class BankPool:

    def __init__(self, b, n=8):
        self.banks = [b.nc.alloc_psum_tensor(f"bank{i}", [128, 512], F32) for i in range(n)]
        self.tok = [None] * n
        self.i = 0
        self.n = n

    def get(self, k=1):
        out = []
        for _ in range(k):
            out.append((self.banks[self.i % self.n], self.i % self.n))
            self.i += 1
        return out

    def wait_tok(self, idx):
        return self.tok[idx]

    def set_tok(self, idx, token):
        self.tok[idx] = token


def wl(*toks):
    return [t for t in toks if t is not None]



def emit_memset_consts(b, sc):
    p = b.prog
    p.emit("vector", "memset", sc.ones_bf[:, :], 1.0)
    p.emit("vector", "memset", sc.ones32[:, :], 1.0)
    sc.emit_const_memsets(b)
    p.emit("vector", "memset", sc.onesr32[:, :], 1.0, inc=[sc.cready.push()])
    return sc.cready.tok()


class Scratch:

    def register_fconst(self, b, *vals):
        for v in vals:
            v = float(v)
            if v in self.fconsts:
                continue
            t = b.sbuf(f"c_{len(self.fconsts)}", [128, 1], F32)
            self.fconsts[v] = t
            b.nc.const_aps.register(F32, v, t[:, :])

    def emit_const_memsets(self, b):
        for v, t in self.fconsts.items():
            b.prog.emit("vector", "memset", t[:, :], v)

    def __init__(self, b, ntok, nbanks=8):
        self.fconsts = {}
        self.ntok = ntok
        self.ones_bf = b.sbuf("ones_bf", [128, 1], BF16)
        self.ones32 = b.sbuf("ones32", [128, 1], F32)
        self.onesr32 = b.sbuf("onesr32", [1, 128], F32)
        self.sq = b.sbuf("scr_sq", [128, ntok], F32)
        self.row = b.sbuf("scr_row", [1, 4 * ntok], F32)
        self.mb = b.sbuf("scr_mb", [128, ntok], F32)
        self.rb = b.sbuf("scr_rb", [128, ntok], F32)
        self.cready = b.chan("cready")
        self.banks = BankPool(b, nbanks)


def emit_layernorm_fm(b, sc, x_sb, xn32_sb, xnb_sb, ntok, tag, wait_x=None,
                      eps=1e-5):
    p = b.prog
    t = b.chan(f"{tag}t")
    v = b.chan(f"{tag}v")
    s = b.chan(f"{tag}s")
    nch = len(x_sb)
    ones = sc.ones32[:, :]
    onesr = sc.onesr32[:, :]
    mean_row = sc.row[0:1, 0:ntok]
    sqm_row = sc.row[0:1, ntok:2 * ntok]
    rstd_row = sc.row[0:1, 2 * ntok:3 * ntok]
    nhalf = (ntok + 511) // 512

    (bk_sx, ix), (bk_sq, iq) = sc.banks.get(2)
    prev_half_tok = None
    for hh in range(nhalf):
        h0, h1 = hh * 512, min(ntok, (hh + 1) * 512)
        for i in range(nch):
            w = wl(sc.banks.wait_tok(ix) if (i == 0 and hh == 0) else None,
                   prev_half_tok if i == 0 else None,
                   *(wait_x if (i == 0 and hh == 0) else []))
            p.emit("tensor", "matmul", bk_sx[0:1, 0:h1 - h0], ones, x_sb[i][:, h0:h1],
                   start=(i == 0), stop=(i == nch - 1), wait=w)
        for i in range(nch):
            p.emit("scalar", "activation", out=sc.sq[:, h0:h1], in_=x_sb[i][:, h0:h1],
                   func=AF.Square,
                   wait=wl(*(wait_x if (i == 0 and hh == 0) else []),
                           (t, t.count) if i > 0 else None,
                           prev_half_tok if i == 0 else None),
                   inc=[s.push()])
            p.emit("tensor", "matmul", bk_sq[0:1, 0:h1 - h0], ones, sc.sq[:, h0:h1],
                   start=(i == 0), stop=(i == nch - 1),
                   wait=wl(sc.banks.wait_tok(iq) if (i == 0 and hh == 0) else None,
                           (s, s.count)),
                   inc=[t.push()])
        done_mm = t.tok()
        p.emit("scalar", "mul", out=mean_row[0:1, h0:h1], in_=bk_sx[0:1, 0:h1 - h0],
               mul=1.0 / D, wait=[done_mm])
        p.emit("scalar", "mul", out=sqm_row[0:1, h0:h1], in_=bk_sq[0:1, 0:h1 - h0],
               mul=1.0 / D, inc=[s.push()])
        prev_half_tok = s.tok()
    stats_tok = s.tok()
    sc.banks.set_tok(ix, stats_tok)
    sc.banks.set_tok(iq, stats_tok)
    p.emit("vector", "scalar_tensor_tensor", out=rstd_row, in0=mean_row,
           scalar=-1.0, in1=mean_row, op0=ALU.mult, op1=ALU.mult,
           wait=[stats_tok], inc=[v.push()])
    p.emit("vector", "tensor_add", out=rstd_row, in0=rstd_row, in1=sqm_row,
           wait=[v.tok()], inc=[v.push()])
    p.emit("scalar", "activation", out=rstd_row, in_=rstd_row, func=AF.Sqrt,
           bias=float(eps), wait=[v.tok()], inc=[s.push()])
    p.emit("vector", "reciprocal", out=rstd_row, in_=rstd_row, wait=[s.tok()],
           inc=[v.push()])
    rstd_tok = v.tok()

    (bk_m, im), (bk_r, ir) = sc.banks.get(2)
    prev_copy = None
    for hh in range(nhalf):
        h0, h1 = hh * 512, min(ntok, (hh + 1) * 512)
        p.emit("tensor", "matmul", bk_m[:, 0:h1 - h0], onesr, mean_row[0:1, h0:h1],
               start=True, stop=True,
               wait=wl(rstd_tok if hh == 0 else None,
                       prev_copy,
                       sc.banks.wait_tok(im) if hh == 0 else None,
                       sc.banks.wait_tok(ir) if hh == 0 else None))
        p.emit("tensor", "matmul", bk_r[:, 0:h1 - h0], onesr, rstd_row[0:1, h0:h1],
               start=True, stop=True, inc=[t.push()])
        btok = t.tok()
        p.emit("scalar", "copy", out=sc.mb[:, h0:h1], in_=bk_m[:, 0:h1 - h0], wait=[btok])
        p.emit("scalar", "copy", out=sc.rb[:, h0:h1], in_=bk_r[:, 0:h1 - h0],
               inc=[s.push()])
        prev_copy = s.tok()
    bc_tok = s.tok()
    sc.banks.set_tok(im, bc_tok)
    sc.banks.set_tok(ir, bc_tok)

    sq_free = [bc_tok]
    for i in range(nch):
        p.emit("vector", "tensor_sub", out=sc.sq[:, 0:ntok], in0=x_sb[i], in1=sc.mb[:, 0:ntok],
               wait=sq_free, inc=[v.push()])
        dst32 = xn32_sb[i] if xn32_sb is not None else sc.sq[:, 0:ntok]
        p.emit("vector", "tensor_mul", out=dst32, in0=sc.sq[:, 0:ntok], in1=sc.rb[:, 0:ntok],
               wait=[v.tok()], inc=[v.push()])
        sq_free = [v.tok()]
        if xnb_sb is not None:
            p.emit("scalar", "copy", out=xnb_sb[i], in_=dst32,
                   wait=[v.tok()], inc=[s.push()])
            if xn32_sb is None:
                sq_free = [s.tok()]
    if xnb_sb is not None:
        return (s.tok(), v.tok())
    return (v.tok(), v.tok())


def emit_gemm_fm(b, sc, tag, w_dram, w_rows, oc_list, xnb, ntok, consume,
                 wait_in=None, grp=2, w_dt=F32, extra_wait_first_mm=None,
                 wprep=None):
    p = b.prog
    t = b.chan(f"{tag}t")
    s = b.chan(f"{tag}s")
    wld2 = [b.chan(f"{tag}w", 16), b.chan(f"{tag}w", 16)]
    nhalf = (ntok + 511) // 512
    w32 = [b.sbuf(f"{tag}_w32", [128, grp * 128], w_dt) for _ in range(2)]
    wb = [b.sbuf(f"{tag}_wb", [128, grp * 128], BF16) for _ in range(2)]
    wfree = [None, None]

    it = 0
    for g0 in range(0, len(oc_list), grp):
        ocs = oc_list[g0:g0 + grp]
        banks = sc.banks.get(len(ocs) * nhalf)
        for f in range(w_rows):
            buf = it % 2
            wld = wld2[buf]
            ow = wl(wfree[buf], *(wait_in if it == 0 else []))
            for j, oc in enumerate(ocs):
                p.dma("gpsimd", out=w32[buf][:, j * 128:(j + 1) * 128],
                      in_=w_dram[f * 128:(f + 1) * 128, oc * 128:(oc + 1) * 128],
                      wait=ow if j == 0 else None, inc=[wld.push()])
            if wprep is None:
                p.emit("scalar", "copy", out=wb[buf][:, 0:len(ocs) * 128],
                       in_=w32[buf][:, 0:len(ocs) * 128],
                       wait=[wld.tok()], inc=[s.push()])
                wcast = s.tok()
            else:
                wcast = wprep(w32[buf][:, 0:len(ocs) * 128],
                              wb[buf][:, 0:len(ocs) * 128], wld.tok())
            for j in range(len(ocs)):
                for hh in range(nhalf):
                    h0, h1 = hh * 512, min(ntok, (hh + 1) * 512)
                    bk, idx = banks[j * nhalf + hh]
                    w = wl(wcast if (j == 0 and hh == 0) else None,
                           sc.banks.wait_tok(idx) if f == 0 else None,
                           extra_wait_first_mm if (it == 0 and j == 0 and hh == 0) else None)
                    last = (j == len(ocs) - 1) and (hh == nhalf - 1)
                    p.emit("tensor", "matmul", bk[:, 0:h1 - h0],
                           wb[buf][:, j * 128:(j + 1) * 128], xnb[f][:, h0:h1],
                           start=(f == 0), stop=(f == w_rows - 1),
                           wait=w, inc=[t.push()] if last else None)
            wfree[buf] = t.tok()
            it += 1
        grp_tok = t.tok()
        for j, oc in enumerate(ocs):
            for hh in range(nhalf):
                h0, h1 = hh * 512, min(ntok, (hh + 1) * 512)
                bk, idx = banks[j * nhalf + hh]
                ctok = consume(oc, h0, h1, bk[:, 0:h1 - h0], grp_tok)
                sc.banks.set_tok(idx, ctok)
    return t.tok()



def build_launch1(ntok=1024):
    b = Builder()
    nc, p = b.nc, b.prog
    x = b.dram("x", [D, ntok], F32, kind="ExternalInput").ap()
    wqkv = b.dram("wqkv", [D, 3 * D], F32, kind="ExternalInput").ap()
    K_out = b.dram("K_out", [D, ntok], BF16, kind="ExternalOutput").ap()
    Q_out = b.dram("Q_out", [D, ntok], BF16, kind="ExternalOutput").ap()
    V_out = b.dram("V_out", [ntok, D], BF16, kind="ExternalOutput").ap()
    norms_out = b.dram("norms_out", [1, 64], F32, kind="ExternalOutput").ap()

    ld = b.chan("ld", 16)
    st = b.chan("st", 16)
    sq_ = b.chan("sqx")
    vq = b.chan("vqx")

    NCH = 16
    x_t = b.sbuf("x", [128, NCH * ntok], F32)
    xs = [x_t[:, i * ntok:(i + 1) * ntok] for i in range(NCH)]
    xnb_t = b.sbuf("xnb", [128, NCH * ntok], BF16)
    xnb = [xnb_t[:, i * ntok:(i + 1) * ntok] for i in range(NCH)]

    sc = Scratch(b, ntok)
    sc.register_fconst(b, 1e-5)
    cready = emit_memset_consts(b, sc)

    for i in range(NCH):
        p.dma("sync", out=xs[i], in_=x[i * 128:(i + 1) * 128, :], inc=[ld.push()])
    x_tok = ld.tok()

    xn_tok, _ = emit_layernorm_fm(b, sc, xs, None, xnb, ntok, "ln1",
                                  wait_x=[x_tok, cready])

    kq_sb = b.sbuf("kq_sb", [128, 4 * 512], BF16)
    kq_slot = [0]
    kq_store_tok = [None] * 4
    kq_st = [b.chan("kqst", 16) for _ in range(4)]
    nsq = b.sbuf("nsq", [128, 512], F32)
    nacc = b.sbuf("nacc", [1, 64], F32)
    b_tq = b.chan("btq")

    def consume_kq(oc, h0, h1, ps, mm_tok):
        is_q = oc < 16
        head = oc if is_q else oc - 16
        stripe_i = h0 // 512
        slot = kq_slot[0] % 4
        kq_slot[0] += 1
        dst = kq_sb[:, slot * 512:slot * 512 + (h1 - h0)]
        p.emit("scalar", "activation", out=dst, in_=ps, func=AF.Copy,
               scale=(RSQ if is_q else 1.0),
               wait=wl(mm_tok, kq_store_tok[slot]), inc=[sq_.push()])
        ev = sq_.tok()
        out_ap = (Q_out if is_q else K_out)[(oc % 16) * 128:(oc % 16) * 128 + 128, h0:h1]
        p.dma("sync", out=out_ap, in_=dst, wait=[ev], inc=[kq_st[slot].push()])
        kq_store_tok[slot] = kq_st[slot].tok()
        p.emit("scalar", "activation", out=nsq[:, 0:h1 - h0], in_=ps, func=AF.Square,
               scale=(RSQ if is_q else 1.0), wait=[b_tq.tok()], inc=[sq_.push()])
        sq_tok = sq_.tok()
        (bk, idx) = sc.banks.get(1)[0]
        p.emit("tensor", "matmul", bk[0:1, 0:h1 - h0], sc.ones32[:, :], nsq[:, 0:h1 - h0],
               start=True, stop=True,
               wait=wl(sq_tok, sc.banks.wait_tok(idx)),
               inc=[b_tq.push()])
        nidx = (0 if is_q else 32) + head * 2 + stripe_i
        p.emit("vector", "reduce_max", nacc[0:1, nidx:nidx + 1], bk[0:1, 0:h1 - h0],
               axis=AX.X, wait=[b_tq.tok()], inc=[vq.push()])
        sc.banks.set_tok(idx, vq.tok())
        return sq_tok

    emit_gemm_fm(b, sc, "qk", wqkv, 16, list(range(32)), xnb, ntok,
                 consume_kq, wait_in=[xn_tok])

    wv32 = b.sbuf("wv32", [128, 512], F32)
    wvb = b.sbuf("wvb", [128, 16 * 512], BF16)
    v_sb = b.sbuf("v_sb", [128, 2 * 512], BF16)
    vwl = b.chan("vwl", 16)
    vslot = [0]
    v_store_tok = [None] * 2
    v_st = [b.chan("vst", 16) for _ in range(2)]
    for cg in range(4):
        for f in range(16):
            p.dma("gpsimd", out=wv32[:, :],
                  in_=wqkv[f * 128:(f + 1) * 128, 2 * D + cg * 512:2 * D + (cg + 1) * 512],
                  wait=[sq_.tok()] if True else None, inc=[vwl.push()])
            p.emit("scalar", "copy", out=wvb[:, f * 512:(f + 1) * 512], in_=wv32[:, :],
                   wait=[vwl.tok()], inc=[sq_.push()])
        wv_tok = sq_.tok()
        for ts in range(ntok // 128):
            (bk, idx) = sc.banks.get(1)[0]
            for f in range(16):
                p.emit("tensor", "matmul", bk[:, 0:512],
                       xnb[f][:, ts * 128:(ts + 1) * 128], wvb[:, f * 512:(f + 1) * 512],
                       start=(f == 0), stop=(f == 15),
                       wait=wl(wv_tok if f == 0 else None,
                               sc.banks.wait_tok(idx) if f == 0 else None),
                       inc=[b_tq.push()] if f == 15 else None)
            slot = vslot[0] % 2
            vslot[0] += 1
            dst = v_sb[:, slot * 512:(slot + 1) * 512]
            p.emit("scalar", "copy", out=dst, in_=bk[:, 0:512],
                   wait=wl(b_tq.tok(), v_store_tok[slot]), inc=[sq_.push()])
            sc.banks.set_tok(idx, sq_.tok())
            p.dma("sync", out=V_out[ts * 128:(ts + 1) * 128, cg * 512:(cg + 1) * 512],
                  in_=dst, wait=[sq_.tok()], inc=[v_st[slot].push()])
            v_store_tok[slot] = v_st[slot].tok()

    p.emit("scalar", "activation", out=nacc[:, :], in_=nacc[:, :], func=AF.Sqrt,
           wait=[vq.tok(), sq_.tok()], inc=[sq_.push()])
    p.dma("sync", out=norms_out, in_=nacc[:, :], wait=[sq_.tok()], inc=[st.push()])
    p.barrier("sync", [st.tok()] + [c.tok() for c in kq_st] + [c.tok() for c in v_st])
    return b.finish()



def tri_masks_np():
    out = []
    for j in range(4):
        kk = np.arange(128)[:, None] + j * 128
        qq = np.arange(512)[None, :]
        out.append((kk <= qq).astype(BF16NP))
    return out


def build_launch2(core, ntok=1024):
    b = Builder()
    nc, p = b.nc, b.prog
    strp = stripes_of(core)
    x = b.dram("x", [D, ntok], F32, kind="ExternalInput").ap()
    Qo = b.dram("Qo", [D, ntok], BF16, kind="ExternalInput").ap()
    Kf = b.dram("Kf", [D, S], BF16, kind="ExternalInput").ap()
    Vf = b.dram("Vf", [S, D], BF16, kind="ExternalInput").ap()
    qnorm = b.dram("qnorm", [1, 32], F32, kind="ExternalInput").ap()
    knorm = b.dram("knorm", [1, 256], F32, kind="ExternalInput").ap()
    wout = b.dram("wout", [D, D], F32, kind="ExternalInput").ap()
    fw = b.dram("fw", [D, D], I8, kind="ExternalInput").ap()
    fsc = b.dram("fsc", [128, 16], F32, kind="ExternalInput").ap()
    fbias = b.dram("fbias", [128, 16], F32, kind="ExternalInput").ap()
    x2_out = b.dram("x2_out", [D, ntok], F32, kind="ExternalOutput").ap()

    ld = b.chan("ld", 16)
    tq = b.chan("tq")
    vq = b.chan("vq")
    sq_ = b.chan("sq")

    NCH = 16
    x_t = b.sbuf("x", [128, NCH * ntok], F32)
    xs = [x_t[:, i * ntok:(i + 1) * ntok] for i in range(NCH)]
    q_t = b.sbuf("q", [128, NCH * ntok], BF16)
    qsb = [q_t[:, i * ntok:(i + 1) * ntok] for i in range(NCH)]
    at_t = b.sbuf("attn", [128, NCH * ntok], BF16)
    atsb = [at_t[:, i * ntok:(i + 1) * ntok] for i in range(NCH)]
    qq_t = q_t
    qqsb = [qq_t[:, i * ntok:(i + 1) * ntok] for i in range(NCH)]

    sc = Scratch(b, ntok, nbanks=4)
    sc.register_fconst(b, 1e-5)
    cready = emit_memset_consts(b, sc)
    ps_st = [nc.alloc_psum_tensor("ps_st0", [128, 512], F32),
             nc.alloc_psum_tensor("ps_st1", [128, 512], F32)]
    ps_acc = nc.alloc_psum_tensor("ps_acc", [128, 512], F32)
    ps_den = nc.alloc_psum_tensor("ps_den", [128, 512], F32)

    masks = tri_masks_np()
    mask_dram = [b.const(f"trimask{j}", masks[j]) for j in range(4)]
    mask_sb = b.sbuf("mask_sb", [128, 4 * 512], BF16)

    for i in range(NCH):
        p.dma("sync", out=xs[i], in_=x[i * 128:(i + 1) * 128, :], inc=[ld.push()])
    for i in range(NCH):
        p.dma("sync", out=qsb[i], in_=Qo[i * 128:(i + 1) * 128, :], inc=[ld.push()])
    for j in range(4):
        p.dma("sync", out=mask_sb[:, j * 512:(j + 1) * 512], in_=mask_dram[j].ap(),
              inc=[ld.push()])
    nrow = b.sbuf("nrow", [1, 512], F32)
    p.dma("sync", out=nrow[0:1, 0:32], in_=qnorm, inc=[ld.push()])
    krow = b.sbuf("krow", [1, 256], F32)
    p.dma("sync", out=krow[0:1, :], in_=knorm, inc=[ld.push()])
    ident32 = b.const("ident128f32", np.eye(128, dtype=np.float32))
    idsb = b.sbuf("idsb", [128, 128], F32)
    p.dma("sync", out=idsb[:, :], in_=ident32.ap(), inc=[ld.push()])
    wsc_sb = b.sbuf("wsc", [128, 16], F32)
    wbi_sb = b.sbuf("wbi", [128, 16], F32)
    p.dma("sync", out=wsc_sb[:, :], in_=fsc, inc=[ld.push()])
    p.dma("sync", out=wbi_sb[:, :], in_=fbias, inc=[ld.push()])
    loads_tok = ld.tok()
    id_tok = loads_tok
    wsb_tok = loads_tok

    for h in range(H):
        for si in range(2):
            s = strp[si]
            p.emit("vector", "reduce_max", nrow[0:1, 32 + h * 2 + si:32 + h * 2 + si + 1],
                   krow[0:1, h * 16:h * 16 + s + 1], axis=AX.X,
                   wait=[loads_tok] if (h == 0 and si == 0) else None,
                   inc=[vq.push()] if (h == H - 1 and si == 1) else None)
    p.emit("vector", "tensor_mul", out=nrow[0:1, 64:96], in0=nrow[0:1, 0:32],
           in1=nrow[0:1, 32:64], wait=[vq.tok()], inc=[vq.push()])
    p.emit("vector", "tensor_scalar_mul", out=nrow[0:1, 64:96], in0=nrow[0:1, 64:96],
           scalar1=-1.0, wait=[vq.tok()], inc=[vq.push()])
    (bk_bias, ib) = sc.banks.get(1)[0]
    p.emit("tensor", "matmul", bk_bias[:, 0:32], sc.onesr32[:, :], nrow[0:1, 64:96],
           start=True, stop=True, wait=wl(vq.tok(), cready, sc.banks.wait_tok(ib)),
           inc=[tq.push()])
    bias_sb = b.sbuf("bias_sb", [128, 32], F32)
    p.emit("scalar", "copy", out=bias_sb[:, :], in_=bk_bias[:, 0:32], wait=[tq.tok()],
           inc=[sq_.push()])
    sc.banks.set_tok(ib, sq_.tok())
    bias_tok = sq_.tok()

    ksup = [b.sbuf("ksup", [128, 512], BF16) for _ in range(2)]
    vsup = [b.sbuf("vsup", [128, 512], BF16) for _ in range(2)]
    kvld = [b.chan("kvld", 16), b.chan("kvld", 16)]
    kv_free = [None, None]
    ptile = [b.sbuf("pt", [128, 512], BF16) for _ in range(2)]
    pt_free = [None, None]
    att_f = b.chan("attf")
    rec_row = b.sbuf("rec", [1, 512], F32)
    rb_sb = b.sbuf("rbc", [128, 512], F32)

    sup_i = [0]
    prev_flash = [None]

    def flash(si, h):
        s = strp[si]
        np_k = 4 * (s + 1)
        nsup = (np_k + 3) // 4
        q_rhs = qsb[h][:, si * 512:(si + 1) * 512]
        for u in range(nsup):
            buf = sup_i[0] % 2
            sup_i[0] += 1
            ch = kvld[buf]
            p.dma("gpsimd", out=ksup[buf][:, :],
                  in_=Kf[h * 128:(h + 1) * 128, u * 512:(u + 1) * 512],
                  wait=wl(kv_free[buf], loads_tok if (si == 0 and h == 0 and u == 0) else None),
                  inc=[ch.push()])
            for j in range(4):
                kt = u * 4 + j
                if kt >= np_k:
                    break
                p.dma("gpsimd", out=vsup[buf][:, j * 128:(j + 1) * 128],
                      in_=Vf[kt * 128:(kt + 1) * 128, h * 128:(h + 1) * 128],
                      inc=[ch.push()])
            kv_tok = ch.tok()
            for j in range(4):
                kt = u * 4 + j
                if kt >= np_k:
                    break
                stbank = ps_st[kt % 2]
                p.emit("tensor", "matmul", stbank[:, 0:512],
                       ksup[buf][:, j * 128:(j + 1) * 128], q_rhs,
                       start=True, stop=True,
                       wait=wl(kv_tok if j == 0 else None,
                               pt_free[kt % 2],
                               bias_tok if (si == 0 and h == 0 and u == 0 and j == 0) else None),
                       inc=[tq.push()])
                qk_tok = tq.tok()
                pt = ptile[kt % 2]
                p.emit("scalar", "activation", out=pt[:, :], in_=stbank[:, 0:512],
                       func=AF.Exp, bias=bias_sb[:, h * 2 + si:h * 2 + si + 1],
                       wait=[qk_tok], inc=[sq_.push()])
                ptok = sq_.tok()
                if kt // 4 == s:
                    p.emit("vector", "tensor_mul", out=pt[:, :], in0=pt[:, :],
                           in1=mask_sb[:, (kt % 4) * 512:(kt % 4 + 1) * 512],
                           wait=[ptok], inc=[att_f.push()])
                    ptok = att_f.tok()
                p.emit("tensor", "matmul", ps_acc[:, 0:512],
                       vsup[buf][:, j * 128:(j + 1) * 128], pt[:, :],
                       start=(kt == 0), stop=(kt == np_k - 1),
                       wait=wl(ptok, prev_flash[0] if kt == 0 else None))
                p.emit("tensor", "matmul", ps_den[0:1, 0:512],
                       sc.ones_bf[:, :], pt[:, :],
                       start=(kt == 0), stop=(kt == np_k - 1),
                       inc=[tq.push()])
                pt_free[kt % 2] = tq.tok()
                if j == 3 or kt == np_k - 1:
                    kv_free[buf] = tq.tok()
        mm_done = tq.tok()
        p.emit("vector", "reciprocal", out=rec_row[0:1, :], in_=ps_den[0:1, 0:512],
               wait=[mm_done], inc=[att_f.push()])
        (bk_r, ir) = sc.banks.get(1)[0]
        p.emit("tensor", "matmul", bk_r[:, 0:512], sc.onesr32[:, :], rec_row[0:1, :],
               start=True, stop=True, wait=wl(att_f.tok(), sc.banks.wait_tok(ir)),
               inc=[tq.push()])
        p.emit("scalar", "copy", out=rb_sb[:, :], in_=bk_r[:, 0:512],
               wait=wl(tq.tok(), prev_flash[0]), inc=[sq_.push()])
        sc.banks.set_tok(ir, sq_.tok())
        p.emit("vector", "tensor_mul", out=atsb[h][:, si * 512:(si + 1) * 512],
               in0=ps_acc[:, 0:512], in1=rb_sb[:, :],
               wait=[sq_.tok()], inc=[att_f.push()])
        prev_flash[0] = att_f.tok()
        return att_f.tok()

    attn_done = None
    for h in range(H):
        for si in range(2):
            attn_done = flash(si, h)

    def consume_outproj(oc, h0, h1, ps, mm_tok):
        p.emit("vector", "tensor_add", out=xs[oc][:, h0:h1], in0=xs[oc][:, h0:h1],
               in1=ps, wait=[mm_tok], inc=[vq.push()])
        return vq.tok()

    emit_gemm_fm(b, sc, "op", wout, 16, list(range(16)), atsb, ntok,
                 consume_outproj, wait_in=[attn_done])
    op_tok = vq.tok()

    ln2_tok, _ = emit_layernorm_fm(b, sc, xs, None, None, ntok, "ln2",
                                   wait_x=[op_tok])

    amax = b.sbuf("amax", [128, ntok], F32)
    tsub = b.sbuf("tsub", [128, ntok], F32)
    tabs = b.sbuf("tabs", [128, ntok], F32)
    for i in range(NCH):
        p.emit("vector", "tensor_sub", out=tsub[:, :], in0=xs[i], in1=sc.mb[:, 0:ntok],
               wait=wl(ln2_tok if i == 0 else None, sq_.tok() if i > 0 else None),
               inc=[vq.push()])
        dst_abs = amax if i == 0 else tabs
        p.emit("scalar", "activation", out=dst_abs[:, :], in_=tsub[:, :], func=AF.Abs,
               wait=[vq.tok()], inc=[sq_.push()])
        if i > 0:
            p.emit("vector", "tensor_max", out=amax[:, :], in0=amax[:, :],
                   in1=tabs[:, :], wait=wl(sq_.tok(), vq.tok()), inc=[vq.push()])
    tree_tok = vq.tok()
    tmax = b.sbuf("tmax", [128, 8], F32)
    for j in range(ntok // 128):
        (bk, it_) = sc.banks.get(1)[0]
        p.emit("tensor", "transpose", bk[:, 0:128], amax[:, j * 128:(j + 1) * 128],
               idsb[:, :],
               wait=wl(tree_tok if j == 0 else None, id_tok if j == 0 else None,
                       sc.banks.wait_tok(it_)),
               inc=[tq.push()])
        p.emit("vector", "reduce_max", tmax[0:1 * 128, j:j + 1], bk[:, 0:128],
               axis=AX.X, wait=[tq.tok()], inc=[vq.push()])
        sc.banks.set_tok(it_, vq.tok())
    arow = b.sbuf("arow", [1, 3 * ntok], F32)
    absr = arow[0:1, 0:ntok]
    amax_dr = b.dram("amax_bounce", [128, 8], F32).ap()
    ld_ax = b.chan("ldax", 16)
    p.dma("sync", out=amax_dr, in_=tmax[:, :], wait=[vq.tok()], inc=[ld_ax.push()])
    p.dma("sync", out=absr,
          in_=amax_dr.rearrange("p a -> a p"),
          wait=[ld_ax.tok()], inc=[ld_ax.push()], allow_slow_non_contiguous=True)
    absr_tok = ld_ax.tok()
    rstd_row = sc.row[0:1, 2 * ntok:3 * ntok]
    gam = arow[0:1, ntok:2 * ntok]
    isc = arow[0:1, 2 * ntok:3 * ntok]
    p.emit("vector", "tensor_mul", out=isc, in0=absr, in1=rstd_row,
           wait=[absr_tok], inc=[vq.push()])
    p.emit("vector", "tensor_scalar", out=isc, in0=isc, scalar1=1e-7, scalar2=1.0 / 127.0,
           op0=ALU.max, op1=ALU.mult, wait=[vq.tok()], inc=[vq.push()])
    isc_tok = vq.tok()
    p.emit("vector", "reciprocal", out=gam, in_=isc, wait=[isc_tok], inc=[vq.push()])
    p.emit("vector", "tensor_mul", out=gam, in0=gam, in1=rstd_row,
           wait=[vq.tok()], inc=[vq.push()])
    gam_tok = vq.tok()
    gb = sc.rb
    for hh in range(ntok // 512):
        (bk, ig) = sc.banks.get(1)[0]
        p.emit("tensor", "matmul", bk[:, 0:512], sc.onesr32[:, :],
               gam[0:1, hh * 512:(hh + 1) * 512], start=True, stop=True,
               wait=wl(gam_tok if hh == 0 else None, sc.banks.wait_tok(ig)),
               inc=[tq.push()])
        p.emit("scalar", "copy", out=gb[:, hh * 512:(hh + 1) * 512], in_=bk[:, 0:512],
               wait=[tq.tok()], inc=[sq_.push()])
        sc.banks.set_tok(ig, sq_.tok())
    gb_tok = sq_.tok()
    for i in range(NCH):
        p.emit("vector", "tensor_sub", out=tsub[:, :], in0=xs[i], in1=sc.mb[:, 0:ntok],
               wait=wl(gb_tok if i == 0 else vq.tok()), inc=[vq.push()])
        p.emit("vector", "tensor_mul", out=tsub[:, :], in0=tsub[:, :], in1=gb[:, 0:ntok],
               wait=[vq.tok()], inc=[vq.push()])
        p.emit("vector", "tensor_scalar", out=qqsb[i], in0=tsub[:, :],
               scalar1=MAGIC, scalar2=MAGIC, op0=ALU.add, op1=ALU.subtract,
               wait=[vq.tok()], inc=[vq.push()])
    qq_tok = vq.tok()

    ib_sb = sc.mb
    for hh in range(ntok // 512):
        (bk, ii) = sc.banks.get(1)[0]
        p.emit("tensor", "matmul", bk[:, 0:512], sc.onesr32[:, :],
               isc[0:1, hh * 512:(hh + 1) * 512], start=True, stop=True,
               wait=wl(isc_tok if hh == 0 else None, sc.banks.wait_tok(ii)),
               inc=[tq.push()])
        p.emit("scalar", "copy", out=ib_sb[:, hh * 512:(hh + 1) * 512], in_=bk[:, 0:512],
               wait=wl(tq.tok(), qq_tok if hh == 0 else None), inc=[sq_.push()])
        sc.banks.set_tok(ii, sq_.tok())
    ib_tok = sq_.tok()

    f16t = b.sbuf("f16t", [128, 512], F16)
    f32t = b.sbuf("f32t", [128, 512], F32)

    def consume_ffn(oc, h0, h1, ps, mm_tok):
        p.emit("vector", "tensor_mul", out=f32t[:, 0:h1 - h0], in0=ps,
               in1=ib_sb[:, h0:h1],
               wait=wl(mm_tok, ib_tok, wsb_tok, sq_.tok(), vq.tok()),
               inc=[vq.push()])
        psum_done = vq.tok()
        p.emit("scalar", "mul", out=f32t[:, 0:h1 - h0], in_=f32t[:, 0:h1 - h0],
               mul=wsc_sb[:, oc:oc + 1], wait=[psum_done], inc=[sq_.push()])
        p.emit("scalar", "add", out=f16t[:, 0:h1 - h0], in_=f32t[:, 0:h1 - h0],
               add=wbi_sb[:, oc:oc + 1], wait=[sq_.tok()], inc=[sq_.push()])
        p.emit("scalar", "copy", out=f32t[:, 0:h1 - h0], in_=f16t[:, 0:h1 - h0],
               wait=[sq_.tok()], inc=[sq_.push()])
        p.emit("vector", "tensor_add", out=xs[oc][:, h0:h1], in0=xs[oc][:, h0:h1],
               in1=f32t[:, 0:h1 - h0], wait=[sq_.tok()], inc=[vq.push()])
        p.dma("sync", out=x2_out[oc * 128:(oc + 1) * 128, h0:h1], in_=xs[oc][:, h0:h1],
              wait=[vq.tok()], inc=[st_ffn[oc % 4].push()])
        return psum_done

    st_ffn = [b.chan("stf", 16) for _ in range(4)]
    emit_gemm_fm(b, sc, "ffn", fw, 16, list(range(16)), qqsb, ntok,
                 consume_ffn, wait_in=[qq_tok], w_dt=I8)

    p.barrier("sync", [c.tok() for c in st_ffn])
    return b.finish()



def build_launch3(ntok=1024):
    NEXT = 1032
    C = 64
    NCHK = ntok // C
    b = Builder()
    nc, p = b.nc, b.prog
    x = b.dram("x", [D, NEXT], F32, kind="ExternalInput").ap()
    wqkv = b.dram("wqkv", [D, 3 * D], F32, kind="ExternalInput").ap()
    wz = b.dram("wz", [D, D], F32, kind="ExternalInput").ap()
    wa = b.dram("wa", [128, 256], F32, kind="ExternalInput").ap()
    wb_ = b.dram("wb", [128, 256], F32, kind="ExternalInput").ap()
    convw = b.dram("convw", [128, 192], F32, kind="ExternalInput").ap()
    alog = b.dram("alog", [16, 1], F32, kind="ExternalInput").ap()
    dtb = b.dram("dtb", [16, 1], F32, kind="ExternalInput").ap()
    o_wloc = b.dram("o_wloc", [NCHK * 16, C * 128], F32, kind="ExternalOutput").ap()
    o_bfm = b.dram("o_bfm", [NCHK * 16, 128 * C], BF16, kind="ExternalOutput").ap()
    o_tmt = b.dram("o_tmt", [NCHK * 16, C * C], BF16, kind="ExternalOutput").ap()
    o_ktm = b.dram("o_ktm", [NCHK * 16, C * 128], BF16, kind="ExternalOutput").ap()
    o_qfm = b.dram("o_qfm", [NCHK * 16, 128 * C], BF16, kind="ExternalOutput").ap()
    o_btm = b.dram("o_btm", [NCHK, C * 16], F32, kind="ExternalOutput").ap()
    o_pptr = b.dram("o_pptr", [2 * 16, 3 * 128 * 128], BF16, kind="ExternalOutput").ap()
    o_zsil = b.dram("o_zsil", [D, ntok], BF16, kind="ExternalOutput").ap()

    ld = b.chan("ld", 16)
    tq = b.chan("tq")
    vq = b.chan("vq")
    sq_ = b.chan("sq")

    NCH = 16
    x_t = b.sbuf("x", [128, NCH * NEXT], F32)
    xs = [x_t[:, i * NEXT:(i + 1) * NEXT] for i in range(NCH)]
    xnb_t = b.sbuf("xnb", [128, NCH * NEXT], BF16)
    xnb = [xnb_t[:, i * NEXT:(i + 1) * NEXT] for i in range(NCH)]

    sc = Scratch(b, NEXT, nbanks=7)
    bfbank = nc.alloc_psum_tensor("bfbank", [128, 1024], BF16)
    bf_tok = [None]
    sc.register_fconst(b, 1e-5, 1e-6, 1.0)
    cready = emit_memset_consts(b, sc)
    ones_full = b.sbuf("ones_full", [128, 128], F32)
    p.emit("vector", "memset", ones_full[:, :], 1.0, inc=[vq.push()])
    ones_tok = vq.tok()

    identb = b.const("identb128", np.eye(128, dtype=BF16NP))
    ident32 = b.const("ident128f32", np.eye(128, dtype=np.float32))
    LT = b.const("LT64", np.triu(np.ones((C, C), dtype=np.float32)))
    aa = np.arange(C)
    MU = np.where(aa[:, None] > aa[None, :], -1e30, 0.0).astype(np.float32)
    MLS = np.where(aa[:, None] <= aa[None, :], 1e30, 0.0).astype(np.float32)
    cMU = b.const("cMU", MU)
    cMLS = b.const("cMLS", MLS)
    idb_sb = b.sbuf("idb", [128, 128], BF16)
    id32_sb = b.sbuf("id32", [128, 128], F32)
    lt_sb = b.sbuf("lt", [64, 64], F32)
    mu_sb = b.sbuf("mu", [64, 2 * 64], F32)
    cw_sb = b.sbuf("cw", [128, 192], F32)
    al_sb = b.sbuf("al", [16, 2], F32)
    wa_sb = b.sbuf("wa_sb", [128, 256], F32)
    wb_sb = b.sbuf("wb_sb", [128, 256], F32)
    p.dma("sync", out=idb_sb[:, :], in_=identb.ap(), inc=[ld.push()])
    p.dma("sync", out=id32_sb[:, :], in_=ident32.ap(), inc=[ld.push()])
    p.dma("sync", out=lt_sb[:, :], in_=LT.ap(), inc=[ld.push()])
    p.dma("sync", out=mu_sb[:, 0:64], in_=cMU.ap(), inc=[ld.push()])
    p.dma("sync", out=mu_sb[:, 64:128], in_=cMLS.ap(), inc=[ld.push()])
    p.dma("sync", out=cw_sb[:, :], in_=convw, inc=[ld.push()])
    p.dma("sync", out=al_sb[:, 0:1], in_=alog, inc=[ld.push()])
    p.dma("sync", out=al_sb[:, 1:2], in_=dtb, inc=[ld.push()])
    p.dma("sync", out=wa_sb[:, :], in_=wa, inc=[ld.push()])
    p.dma("sync", out=wb_sb[:, :], in_=wb_, inc=[ld.push()])
    for i in range(NCH):
        p.dma("sync", out=xs[i], in_=x[i * 128:(i + 1) * 128, :], inc=[ld.push()])
    loads = ld.tok()

    negA = b.sbuf("negA", [16, 2], F32)
    p.emit("scalar", "activation", out=negA[:, 0:1], in_=al_sb[:, 0:1], func=AF.Exp,
           wait=[loads], inc=[sq_.push()])
    p.emit("vector", "tensor_scalar_mul", out=negA[:, 0:1], in0=negA[:, 0:1],
           scalar1=-1.0, wait=[sq_.tok()], inc=[vq.push()])
    negA_tok = vq.tok()

    xn_tok, xn32_tok = emit_layernorm_fm(b, sc, xs, xs, xnb, NEXT, "ln3",
                                         wait_x=[loads, cready])

    g_fm = b.sbuf("g_fm", [16, ntok], F32)
    beta_fm = b.sbuf("beta_fm", [16, ntok], F32)
    ab_prev = [None]
    (bka, ia), (bkb, ib4) = sc.banks.get(2)
    for hh in range(2):
        h0, h1 = hh * 512, (hh + 1) * 512
        for f in range(16):
            p.emit("tensor", "matmul", bka[0:16, 0:512], wa_sb[:, f * 16:(f + 1) * 16],
                   xs[f][:, h0:h1], start=(f == 0), stop=(f == 15),
                   wait=wl(xn32_tok if (hh == 0 and f == 0) else None,
                           negA_tok if (hh == 0 and f == 0) else None,
                           sc.banks.wait_tok(ia) if (hh == 0 and f == 0) else None,
                           ab_prev[0] if f == 0 else None))
            p.emit("tensor", "matmul", bkb[0:16, 0:512], wb_sb[:, f * 16:(f + 1) * 16],
                   xs[f][:, h0:h1], start=(f == 0), stop=(f == 15),
                   wait=wl(sc.banks.wait_tok(ib4) if (hh == 0 and f == 0) else None),
                   inc=[tq.push()] if f == 15 else None)
        mm_tok = tq.tok()
        p.emit("scalar", "activation", out=g_fm[0:16, h0:h1], in_=bka[0:16, 0:512],
               func=AF.Exp, bias=al_sb[:, 1:2], wait=[mm_tok], inc=[sq_.push()])
        p.emit("scalar", "activation", out=g_fm[0:16, h0:h1], in_=g_fm[0:16, h0:h1],
               func=AF.Ln, bias=1.0, wait=[sq_.tok()], inc=[sq_.push()])
        p.emit("scalar", "mul", out=g_fm[0:16, h0:h1], in_=g_fm[0:16, h0:h1],
               mul=negA[:, 0:1], wait=[sq_.tok()], inc=[sq_.push()])
        p.emit("scalar", "activation", out=beta_fm[0:16, h0:h1], in_=bkb[0:16, 0:512],
               func=AF.Sigmoid, inc=[sq_.push()])
        ab_prev[0] = sq_.tok()
    gb_tok = sq_.tok()
    sc.banks.set_tok(ia, gb_tok)
    sc.banks.set_tok(ib4, gb_tok)

    gt_t = b.sbuf("gt", [64, NCHK * 16], F32)
    bt_t = b.sbuf("bt", [64, NCHK * 16], F32)
    nbt_t = b.sbuf("nbt", [64, NCHK * 16], F32)
    btm_t = b.sbuf("btm", [64, NCHK * 16], F32)
    gam_t = b.sbuf("gam", [64, NCHK * 16], F32)
    bg_t = b.sbuf("bg", [64, NCHK * 16], F32)
    sc1_t = b.sbuf("sc1", [64, NCHK * 16], F32)
    bcr_t = b.sbuf("bcr", [1, NCHK * 16], F32)
    gcr_t = b.sbuf("gcr", [1, NCHK * 16], F32)
    ones64 = b.sbuf("ones64", [64, 64], F32)
    chunk_gate_tok = [None] * NCHK
    st_btm = b.chan("stbtm", 16)
    bcr_ld = b.chan("bcrld", 16)
    p.emit("vector", "memset", ones64[:, :], 1.0, inc=[vq.push()])
    for ck in range(NCHK):
        cs = ck * 16
        (bk1, i1) = sc.banks.get(1)[0]
        p.emit("tensor", "transpose", bk1[0:64, 0:16], g_fm[0:16, ck * C:(ck + 1) * C],
               id32_sb[0:16, 0:16],
               wait=wl(gb_tok if ck == 0 else None, sc.banks.wait_tok(i1)))
        p.emit("tensor", "transpose", bk1[0:64, 16:32], beta_fm[0:16, ck * C:(ck + 1) * C],
               id32_sb[0:16, 0:16], inc=[tq.push()])
        tr_tok = tq.tok()
        p.emit("scalar", "copy", out=gt_t[:, cs:cs + 16], in_=bk1[0:64, 0:16],
               wait=[tr_tok], inc=[sq_.push()])
        p.emit("scalar", "copy", out=bt_t[:, cs:cs + 16], in_=bk1[0:64, 16:32],
               inc=[sq_.push()])
        cp_tok = sq_.tok()
        p.emit("vector", "tensor_scalar_mul", out=nbt_t[:, cs:cs + 16],
               in0=bt_t[:, cs:cs + 16], scalar1=-1.0, wait=[cp_tok], inc=[vq.push()])
        sc.banks.set_tok(i1, cp_tok)
        (bk2, i2) = sc.banks.get(1)[0]
        p.emit("tensor", "matmul", bk2[0:64, 0:16], lt_sb[:, :], gt_t[:, cs:cs + 16],
               start=True, stop=True, wait=wl(cp_tok, sc.banks.wait_tok(i2)),
               inc=[tq.push()])
        p.emit("scalar", "copy", out=btm_t[:, cs:cs + 16], in_=bk2[0:64, 0:16],
               wait=[tq.tok()], inc=[sq_.push()])
        btok = sq_.tok()
        sc.banks.set_tok(i2, btok)
        p.dma("sync", out=o_btm[ck:ck + 1, :], in_=btm_t[:, cs:cs + 16], wait=[btok],
              inc=[st_btm.push()])
        p.dma("gpsimd", out=bcr_t[0:1, cs:cs + 16], in_=btm_t[63:64, cs:cs + 16],
              wait=[btok], inc=[bcr_ld.push()])
        bcr_tok = bcr_ld.tok()
        p.emit("scalar", "activation", out=gcr_t[0:1, cs:cs + 16],
               in_=bcr_t[0:1, cs:cs + 16], func=AF.Exp, wait=[bcr_tok],
               inc=[sq_.push()])
        p.emit("scalar", "activation", out=gam_t[:, cs:cs + 16], in_=btm_t[:, cs:cs + 16],
               func=AF.Exp, wait=[btok], inc=[sq_.push()])
        p.emit("vector", "tensor_mul", out=bg_t[:, cs:cs + 16], in0=bt_t[:, cs:cs + 16],
               in1=gam_t[:, cs:cs + 16], wait=[sq_.tok(), vq.tok()], inc=[vq.push()])
        (bk3, i3) = sc.banks.get(1)[0]
        p.emit("tensor", "matmul", bk3[0:64, 0:16], sc.onesr32[0:1, 0:64],
               bcr_t[0:1, cs:cs + 16], start=True, stop=True,
               wait=wl(bcr_tok, sc.banks.wait_tok(i3)),
               inc=[tq.push()])
        p.emit("vector", "tensor_sub", out=sc1_t[:, cs:cs + 16], in0=bk3[0:64, 0:16],
               in1=btm_t[:, cs:cs + 16], wait=wl(tq.tok(), vq.tok()), inc=[vq.push()])
        p.emit("scalar", "activation", out=sc1_t[:, cs:cs + 16], in_=sc1_t[:, cs:cs + 16],
               func=AF.Exp, wait=[vq.tok()], inc=[sq_.push()])
        chunk_gate_tok[ck] = (sq_.tok(), vq.tok())
        sc.banks.set_tok(i3, sq_.tok())

    qrot = [b.sbuf("qrot", [128, ntok], BF16) for _ in range(2)]
    krot = [b.sbuf("krot", [128, ntok], BF16) for _ in range(2)]
    vrot = [b.sbuf("vrot", [128, ntok], BF16) for _ in range(2)]
    qfm = [qrot[h % 2] for h in range(16)]
    kfm = [krot[h % 2] for h in range(16)]

    cvt = b.sbuf("cvt", [128, NEXT], F32)
    cv2 = b.sbuf("cv2", [128, NEXT], F32)
    sumsq = b.sbuf("sumsq", [1, 2 * ntok], F32)
    qk32 = b.sbuf("qk32", [128, ntok], F32)
    qk_f32 = [qk32, qk32]
    qdone = [None] * 16
    kdone = [None] * 16
    conv_prev = [None]
    cv2_free = [None]

    def emit_conv(oc, pre_tok, dst_ap, extra=()):
        wj = [cw_sb[:, oc * 4 + j:oc * 4 + j + 1] for j in range(4)]
        for sblk in range(2):
            c0 = sblk * 512
            hstart = 1024 + sblk * 3
            p.emit("vector", "tensor_scalar_mul", out=cvt[:, c0 + 3:c0 + 512],
                   in0=cv2[:, c0:c0 + 509], scalar1=wj[0],
                   wait=wl(pre_tok if sblk == 0 else None,
                           conv_prev[0] if sblk == 0 else vq.tok()),
                   inc=[vq.push()])
            for j in range(1, 4):
                p.emit("vector", "scalar_tensor_tensor", out=cvt[:, c0 + 3:c0 + 512],
                       in0=cv2[:, c0 + j:c0 + 509 + j], scalar=wj[j],
                       in1=cvt[:, c0 + 3:c0 + 512], op0=ALU.mult, op1=ALU.add,
                       wait=[vq.tok()], inc=[vq.push()])
            p.emit("vector", "tensor_scalar_mul", out=cvt[:, c0:c0 + 3],
                   in0=cv2[:, hstart:hstart + 3], scalar1=wj[0],
                   wait=[vq.tok()], inc=[vq.push()])
            for j in range(1, 4):
                if 3 - j > 0:
                    p.emit("vector", "scalar_tensor_tensor", out=cvt[:, c0:c0 + 3 - j],
                           in0=cv2[:, hstart + j:hstart + 3], scalar=wj[j],
                           in1=cvt[:, c0:c0 + 3 - j], op0=ALU.mult, op1=ALU.add,
                           wait=[vq.tok()], inc=[vq.push()])
                p.emit("vector", "scalar_tensor_tensor", out=cvt[:, c0 + 3 - j:c0 + 3],
                       in0=cv2[:, c0:c0 + j], scalar=wj[j],
                       in1=cvt[:, c0 + 3 - j:c0 + 3], op0=ALU.mult, op1=ALU.add,
                       wait=[vq.tok()], inc=[vq.push()])
        conv_tok = vq.tok()
        p.emit("scalar", "activation", out=dst_ap, in_=cvt[:, 0:ntok],
               func=AF.Silu, wait=wl(conv_tok, *extra), inc=[sq_.push()])
        conv_prev[0] = sq_.tok()
        return sq_.tok()

    def emit_l2norm(kind, h, silu_tok):
        src = qk_f32[kind]
        p.emit("scalar", "activation", out=cvt[:, 0:ntok], in_=src[:, 0:ntok],
               func=AF.Square, wait=wl(silu_tok), inc=[sq_.push()])
        sq_tok = sq_.tok()
        conv_prev[0] = sq_tok
        (bk, ib2) = sc.banks.get(1)[0]
        for hh in range(2):
            p.emit("tensor", "matmul", bk[0:1, 0:512], sc.ones32[:, :],
                   cvt[:, hh * 512:(hh + 1) * 512], start=True, stop=True,
                   wait=wl(sq_tok if hh == 0 else (sq_, sq_.count),
                           sc.banks.wait_tok(ib2) if hh == 0 else None),
                   inc=[tq.push()])
            row = sumsq[0:1, kind * ntok + hh * 512: kind * ntok + hh * 512 + 512]
            p.emit("scalar", "copy", out=row, in_=bk[0:1, 0:512],
                   wait=[tq.tok()], inc=[sq_.push()])
        sc.banks.set_tok(ib2, sq_.tok())
        rowfull = sumsq[0:1, kind * ntok:kind * ntok + ntok]
        p.emit("scalar", "activation", out=rowfull, in_=rowfull, func=AF.Sqrt,
               bias=1e-6, wait=[sq_.tok()], inc=[sq_.push()])
        p.emit("vector", "reciprocal", out=rowfull, in_=rowfull, wait=[sq_.tok()],
               inc=[vq.push()])
        if kind == 0:
            p.emit("vector", "tensor_scalar_mul", out=rowfull, in0=rowfull,
                   scalar1=RSQ, wait=[vq.tok()], inc=[vq.push()])
        r_tok = vq.tok()
        dstb = (qfm if kind == 0 else kfm)[h][:, 0:ntok]
        for hh in range(2):
            (bk2, ib3) = sc.banks.get(1)[0]
            p.emit("tensor", "matmul", bk2[:, 0:512], sc.onesr32[:, :],
                   rowfull[0:1, hh * 512:(hh + 1) * 512], start=True, stop=True,
                   wait=wl(r_tok if hh == 0 else None, sc.banks.wait_tok(ib3)),
                   inc=[tq.push()])
            p.emit("vector", "tensor_mul", out=dstb[:, hh * 512:(hh + 1) * 512],
                   in0=src[:, hh * 512:(hh + 1) * 512], in1=bk2[:, 0:512],
                   wait=wl(tq.tok(), vq.tok(), *(rot_free[0] if hh == 0 else [])),
                   inc=[vq.push()])
            sc.banks.set_tok(ib3, vq.tok())
        if kind == 0:
            qdone[h] = vq.tok()
        else:
            kdone[h] = vq.tok()

    ktm = b.sbuf("ktm", [64, 128], BF16)
    vtm = b.sbuf("vtm", [64, 128], BF16)
    kgt = b.sbuf("kgt", [64, 128], BF16)
    rhs2 = b.sbuf("rhs2", [64, 256], BF16)
    dp = b.sbuf("dp", [64, 64], F32)
    cbt = b.sbuf("cbt", [64, 64], F32)
    eL = b.sbuf("eL", [64, 64], F32)
    eU = b.sbuf("eU", [64, 64], F32)
    ntl = b.sbuf("ntl", [64, 64], BF16)
    npw = [b.sbuf("npw", [64, 64], BF16) for _ in range(2)]
    nqw = [b.sbuf("nqw", [64, 64], BF16) for _ in range(2)]
    app = b.sbuf("app", [64, 256], F32)
    appb = b.sbuf("appb", [64, 256], BF16)
    tmt_sb = b.sbuf("tmt_sb", [64, 64], BF16)
    bfm_sb = b.sbuf("bfm_sb", [128, 64], BF16)
    wtmp = b.sbuf("wtmp", [64, 64], F32)
    gcb = b.sbuf("gcb", [128, 1], F32)
    ptmp = b.sbuf("ptmp", [128, 128], F32)
    pptr_acc = b.sbuf("pptr_acc", [128, 3 * 128], BF16)
    pptr_c = b.sbuf("pptr_c", [128, 3 * 128], BF16)
    cm_prev = [[]]
    pacc_tok = [[]]
    pacc_store = [None]
    st_cm = {k: b.chan("stcm", 16) for k in ["wloc", "bfm", "tmt", "ktm", "qfm", "pptr"]}

    def one_chunk(h, ck, vbuf):
        cs = ck * 16
        t0 = ck * C
        row = ck * 16 + h
        si = ck // 8
        kch = kfm[h][:, t0:t0 + C]
        qch = qfm[h][:, t0:t0 + C]
        vch = vrot[vbuf][:, t0:t0 + C]
        prev = cm_prev[0]
        gst, gvt = chunk_gate_tok[ck]
        bcol = btm_t[:, cs + h:cs + h + 1]
        p.emit("tensor", "transpose", bfbank[0:64, 0:128], kch, idb_sb[:, :],
               wait=wl(*prev, bf_tok[0], gvt,
                       kdone[h] if ck == 0 else None,
                       qdone[h] if ck == 0 else None,
                       vready[h][1] if ck == 0 else None),
               inc=[tq.push()])
        p.emit("scalar", "copy", out=ktm[:, :], in_=bfbank[0:64, 0:128],
               wait=[tq.tok()], inc=[sq_.push()])
        p.emit("tensor", "transpose", bfbank[0:64, 128:256], vch, idb_sb[:, :],
               inc=[tq.push()])
        p.emit("scalar", "copy", out=vtm[:, :], in_=bfbank[0:64, 128:256],
               wait=[tq.tok()], inc=[sq_.push()])
        tv_tok = sq_.tok()
        p.emit("scalar", "mul", out=kgt[:, :], in_=ktm[:, :],
               mul=sc1_t[:, cs + h:cs + h + 1], wait=[tv_tok], inc=[sq_.push()])
        p.emit("scalar", "mul", out=rhs2[:, 0:128], in_=vtm[:, :],
               mul=bt_t[:, cs + h:cs + h + 1], inc=[sq_.push()])
        p.emit("scalar", "mul", out=rhs2[:, 128:256], in_=ktm[:, :],
               mul=bg_t[:, cs + h:cs + h + 1], inc=[sq_.push()])
        sc_tok = sq_.tok()
        p.emit("scalar", "mul", out=cbt[:, :], in_=ones64[:, :], mul=bcol,
               wait=wl(*prev), inc=[sq_.push()])
        cb_tok = sq_.tok()
        (bkd, id1) = sc.banks.get(1)[0]
        p.emit("tensor", "transpose", bkd[0:64, 0:64], cbt[:, :], id32_sb[0:64, 0:64],
               wait=wl(cb_tok, sc.banks.wait_tok(id1)), inc=[tq.push()])
        p.emit("vector", "tensor_scalar", out=dp[:, :], in0=bkd[0:64, 0:64],
               scalar1=bcol, scalar2=None, op0=ALU.subtract,
               wait=wl(tq.tok()), inc=[vq.push()])
        dp_tok = vq.tok()
        sc.banks.set_tok(id1, dp_tok)
        p.emit("vector", "tensor_add", out=eL[:, :], in0=dp[:, :], in1=mu_sb[:, 64:128],
               wait=[dp_tok], inc=[vq.push()])
        p.emit("vector", "tensor_add", out=eU[:, :], in0=dp[:, :], in1=mu_sb[:, 0:64],
               wait=[vq.tok()], inc=[vq.push()])
        eadd_tok = vq.tok()
        p.emit("scalar", "activation", out=eL[:, :], in_=eL[:, :], func=AF.Exp,
               scale=-1.0, wait=[eadd_tok], inc=[sq_.push()])
        p.emit("scalar", "activation", out=eU[:, :], in_=eU[:, :], func=AF.Exp,
               wait=[eadd_tok], inc=[sq_.push()])
        exp_tok = sq_.tok()
        (bkk, ik) = sc.banks.get(1)[0]
        p.emit("tensor", "matmul", bkk[0:64, 0:64], kch, kch, start=True, stop=True,
               wait=wl(sc.banks.wait_tok(ik)))
        p.emit("tensor", "matmul", bkk[0:64, 64:128], kch, qch, start=True, stop=True,
               inc=[tq.push()])
        kkx_tok = tq.tok()
        p.emit("vector", "tensor_mul", out=wtmp[:, :], in0=eL[:, :], in1=bkk[0:64, 0:64],
               wait=wl(kkx_tok, exp_tok), inc=[vq.push()])
        p.emit("vector", "tensor_scalar_mul", out=ntl[:, :], in0=wtmp[:, :],
               scalar1=nbt_t[:, cs + h:cs + h + 1], wait=[vq.tok()], inc=[vq.push()])
        ntl_tok = vq.tok()
        p.emit("vector", "tensor_mul", out=tmt_sb[:, :], in0=eU[:, :],
               in1=bkk[0:64, 64:128], wait=[vq.tok()], inc=[vq.push()])
        tmt_tok = vq.tok()
        sc.banks.set_tok(ik, tmt_tok)
        p.dma("sync", out=o_tmt[row:row + 1, :], in_=tmt_sb[:, :], wait=[tmt_tok],
              inc=[st_cm["tmt"].push()])
        p.emit("tensor", "transpose", bfbank[0:64, 256:320], ntl[:, :],
               idb_sb[0:64, 0:64], wait=wl(ntl_tok), inc=[tq.push()])
        p.emit("scalar", "copy", out=npw[0][:, :], in_=bfbank[0:64, 256:320],
               wait=[tq.tok()], inc=[sq_.push()])
        np0_tok = sq_.tok()
        p.emit("scalar", "copy", out=app[:, :], in_=rhs2[:, :], wait=[sc_tok],
               inc=[sq_.push()])
        app_tok = sq_.tok()
        appb_tok = sc_tok
        np_cur = npw[0]
        np_cur_tok = np0_tok
        nq_cur = ntl
        nq_cur_tok = ntl_tok
        for k in range(6):
            (bky, iy) = sc.banks.get(1)[0]
            rhs_ap = rhs2[:, :] if k == 0 else appb[:, :]
            p.emit("tensor", "matmul", bky[0:64, 0:256], np_cur[:, :], rhs_ap,
                   start=True, stop=True,
                   wait=wl(np_cur_tok, appb_tok, sc.banks.wait_tok(iy)),
                   inc=[tq.push()])
            y_tok = tq.tok()
            p.emit("vector", "tensor_add", out=app[:, :], in0=app[:, :],
                   in1=bky[0:64, 0:256], wait=wl(y_tok, app_tok), inc=[vq.push()])
            app_tok = vq.tok()
            sc.banks.set_tok(iy, app_tok)
            p.emit("scalar", "copy", out=appb[:, :], in_=app[:, :], wait=[app_tok],
                   inc=[sq_.push()])
            appb_tok = sq_.tok()
            if k < 5:
                (bkp, ip2) = sc.banks.get(1)[0]
                p.emit("tensor", "matmul", bkp[0:64, 0:64], nq_cur[:, :], np_cur[:, :],
                       start=True, stop=True, wait=wl(sc.banks.wait_tok(ip2)))
                p.emit("tensor", "matmul", bkp[0:64, 64:128], np_cur[:, :], nq_cur[:, :],
                       start=True, stop=True, inc=[tq.push()])
                pw_tok = tq.tok()
                p.emit("scalar", "copy", out=npw[(k + 1) % 2][:, :], in_=bkp[0:64, 0:64],
                       wait=[pw_tok], inc=[sq_.push()])
                p.emit("scalar", "copy", out=nqw[(k + 1) % 2][:, :], in_=bkp[0:64, 64:128],
                       inc=[sq_.push()])
                np_cur = npw[(k + 1) % 2]
                nq_cur = nqw[(k + 1) % 2]
                np_cur_tok = sq_.tok()
                nq_cur_tok = np_cur_tok
                sc.banks.set_tok(ip2, np_cur_tok)
        p.dma("sync", out=o_wloc[row:row + 1, :], in_=app[:, 0:128], wait=[app_tok],
              inc=[st_cm["wloc"].push()])
        p.emit("tensor", "transpose", bfbank[0:128, 320:384], appb[:, 128:256],
               idb_sb[0:64, 0:64],
               wait=wl(appb_tok), inc=[tq.push()])
        p.emit("scalar", "copy", out=bfm_sb[:, :], in_=bfbank[0:128, 320:384],
               wait=[tq.tok()], inc=[sq_.push()])
        bfm_tok = sq_.tok()
        bf_tok[0] = bfm_tok
        p.dma("sync", out=o_bfm[row:row + 1, :], in_=bfm_sb[:, :], wait=[bfm_tok],
              inc=[st_cm["bfm"].push()])
        p.dma("sync", out=o_ktm[row:row + 1, :], in_=kgt[:, :], wait=[sc_tok],
              inc=[st_cm["ktm"].push()])
        p.dma("sync", out=o_qfm[row:row + 1, :], in_=qch, wait=wl(qdone[h] if ck == 0 else None),
              inc=[st_cm["qfm"].push()])
        (bkg, ig2) = sc.banks.get(1)[0]
        p.emit("tensor", "matmul", bkg[0:128, 0:1], sc.onesr32[:, :],
               gcr_t[0:1, cs + h:cs + h + 1], start=True, stop=True,
               wait=wl(sc.banks.wait_tok(ig2)), inc=[tq.push()])
        p.emit("scalar", "copy", out=gcb[:, :], in_=bkg[0:128, 0:1], wait=[tq.tok()],
               inc=[sq_.push()])
        gcb_tok = sq_.tok()
        sc.banks.set_tok(ig2, gcb_tok)
        p.emit("scalar", "mul", out=ptmp[:, :], in_=id32_sb[:, :], mul=gcb[:, 0:1],
               wait=[gcb_tok], inc=[sq_.push()])
        ptmp_tok = sq_.tok()
        (bkp1, ip3), (bkp2, ip4), (bkp3, ip5) = sc.banks.get(3)
        p.emit("tensor", "matmul", bkp1[0:128, 0:128], kgt[:, :], appb[:, 128:256],
               start=True, stop=True, wait=wl(sc.banks.wait_tok(ip3)))
        p.emit("tensor", "matmul", bkp2[0:128, 0:128], appb[:, 128:256], kgt[:, :],
               start=True, stop=True, wait=wl(sc.banks.wait_tok(ip4)))
        p.emit("tensor", "matmul", bkp3[0:128, 0:128], kgt[:, :], appb[:, 0:128],
               start=True, stop=True, wait=wl(sc.banks.wait_tok(ip5)),
               inc=[tq.push()])
        ppr_mm = tq.tok()
        p.emit("vector", "tensor_sub", out=pptr_c[:, 0:128], in0=ptmp[:, :],
               in1=bkp1[0:128, 0:128], wait=wl(ppr_mm, ptmp_tok), inc=[vq.push()])
        p.emit("vector", "tensor_sub", out=pptr_c[:, 128:256], in0=ptmp[:, :],
               in1=bkp2[0:128, 0:128], wait=[vq.tok()], inc=[vq.push()])
        p.emit("scalar", "copy", out=pptr_c[:, 256:384], in_=bkp3[0:128, 0:128],
               wait=[ppr_mm], inc=[sq_.push()])
        ppr_tok = (vq.tok(), sq_.tok())
        sc.banks.set_tok(ip3, vq.tok())
        sc.banks.set_tok(ip4, vq.tok())
        sc.banks.set_tok(ip5, sq_.tok())
        if ck % 8 == 0:
            p.emit("scalar", "copy", out=pptr_acc[:, :], in_=pptr_c[:, :],
                   wait=wl(*ppr_tok, pacc_store[0]), inc=[sq_.push()])
            acc_tok = [sq_.tok()]
        else:
            (bq1, iq1), (bq2, iq2), (bq3, iq3) = sc.banks.get(3)
            p.emit("tensor", "matmul", bq1[0:128, 0:128], pptr_c[:, 128:256],
                   pptr_acc[:, 0:128], start=True, stop=True,
                   wait=wl(*ppr_tok, *pacc_tok[0], sc.banks.wait_tok(iq1)))
            p.emit("tensor", "matmul", bq2[0:128, 0:128], pptr_acc[:, 0:128],
                   pptr_c[:, 128:256], start=True, stop=True,
                   wait=wl(sc.banks.wait_tok(iq2)))
            p.emit("tensor", "matmul", bq3[0:128, 0:128], pptr_c[:, 128:256],
                   pptr_acc[:, 256:384], start=True, stop=True,
                   wait=wl(sc.banks.wait_tok(iq3)), inc=[tq.push()])
            cmp_mm = tq.tok()
            p.emit("scalar", "copy", out=pptr_acc[:, 0:128], in_=bq1[0:128, 0:128],
                   wait=[cmp_mm], inc=[sq_.push()])
            p.emit("scalar", "copy", out=pptr_acc[:, 128:256], in_=bq2[0:128, 0:128],
                   inc=[sq_.push()])
            p.emit("vector", "tensor_add", out=pptr_acc[:, 256:384],
                   in0=pptr_c[:, 256:384], in1=bq3[0:128, 0:128],
                   wait=[cmp_mm], inc=[vq.push()])
            sc.banks.set_tok(iq1, sq_.tok())
            sc.banks.set_tok(iq2, sq_.tok())
            sc.banks.set_tok(iq3, vq.tok())
            acc_tok = [sq_.tok(), vq.tok()]
        pacc_tok[0] = acc_tok
        if ck % 8 == 7:
            w = [t for t in pacc_tok[0] if t is not None]
            p.dma("sync", out=o_pptr[si * 16 + h:si * 16 + h + 1, :],
                  in_=pptr_acc[:, :], wait=w, inc=[st_cm["pptr"].push()])
            pacc_store[0] = st_cm["pptr"].tok()
        cm_prev[0] = [sq_.tok(), vq.tok(), tq.tok(),
                      st_cm["wloc"].tok(), st_cm["bfm"].tok(), st_cm["tmt"].tok(),
                      st_cm["ktm"].tok(), st_cm["qfm"].tok()]

    def consume_qkv(oc, h0, h1, ps, mm_tok):
        p.emit("scalar", "copy", out=cv2[:, h0:h1], in_=ps,
               wait=wl(mm_tok, cv2_free[0] if h0 == 0 else None), inc=[sq_.push()])
        tok = sq_.tok()
        pre_parts.setdefault(oc, []).append(tok)
        if len(pre_parts[oc]) == 3:
            h = oc % 16
            kind = oc // 16
            if kind == 2:
                buf = h % 2
                stok = emit_conv(oc, tok, vrot[buf][:, 0:ntok], extra=rot_free[0])
                vready[h] = (buf, stok)
                for ck in range(NCHK):
                    one_chunk(h, ck, buf)
                rot_free[0] = list(cm_prev[0])
            else:
                stok = emit_conv(oc, tok, qk_f32[kind][:, 0:ntok], extra=[])
                emit_l2norm(kind, h, stok)
            cv2_free[0] = conv_prev[0]
        return tok

    pre_parts = {}
    vready = [None] * 16
    rot_free = [[]]

    qkv_order = []
    for h in range(16):
        qkv_order += [h, 16 + h, 32 + h]
    emit_gemm_fm(b, sc, "qkv", wqkv, 16, qkv_order, xnb, NEXT,
                 consume_qkv, wait_in=[xn_tok], grp=1)

    zst = b.sbuf("zst", [128, 512], BF16)
    z_store = [None]
    st_z = b.chan("stz", 16)

    def consume_z(oc, h0, h1, ps, mm_tok):
        if h0 >= ntok:
            return mm_tok
        h1c = min(h1, ntok)
        p.emit("scalar", "activation", out=zst[:, 0:h1c - h0], in_=ps,
               func=AF.Silu, wait=wl(mm_tok, z_store[0]), inc=[sq_.push()])
        tok = sq_.tok()
        p.dma("sync", out=o_zsil[oc * 128:(oc + 1) * 128, h0:h1c],
              in_=zst[:, 0:h1c - h0], wait=[tok], inc=[st_z.push()])
        z_store[0] = st_z.tok()
        return tok

    emit_gemm_fm(b, sc, "z", wz, 16, list(range(16)), xnb, NEXT, consume_z,
                 wait_in=[xn_tok])

    p.barrier("sync", [st_btm.tok(), st_z.tok()] + [c.tok() for c in st_cm.values()])
    return b.finish()



def build_launch4(core, ntok=1024):
    C = 64
    NCHK = ntok // C
    strp = stripes_of(core)
    b = Builder()
    nc, p = b.nc, b.prog
    x = b.dram("x", [D, ntok], F32, kind="ExternalInput").ap()
    g_pptr = b.dram("g_pptr", [NSTRIPE * 16, 3 * 128 * 128], BF16, kind="ExternalInput").ap()
    i_wloc = b.dram("i_wloc", [NCHK * 16, C * 128], F32, kind="ExternalInput").ap()
    i_bfm = b.dram("i_bfm", [NCHK * 16, 128 * C], BF16, kind="ExternalInput").ap()
    i_tmt = b.dram("i_tmt", [NCHK * 16, C * C], BF16, kind="ExternalInput").ap()
    i_ktm = b.dram("i_ktm", [NCHK * 16, C * 128], BF16, kind="ExternalInput").ap()
    i_qfm = b.dram("i_qfm", [NCHK * 16, 128 * C], BF16, kind="ExternalInput").ap()
    i_btm = b.dram("i_btm", [NCHK, C * 16], F32, kind="ExternalInput").ap()
    i_zsil = b.dram("i_zsil", [D, ntok], BF16, kind="ExternalInput").ap()
    wout = b.dram("wout", [D, D], F32, kind="ExternalInput").ap()
    nw = b.dram("nw", [128, 1], F32, kind="ExternalInput").ap()
    gnw = b.dram("gnw", [128, 16], F32, kind="ExternalInput").ap()
    x3_out = b.dram("x3_out", [D, ntok], F32, kind="ExternalOutput").ap()
    absloc = b.dram("absloc", [1, 8], F32, kind="ExternalOutput").ap()

    ld = b.chan("ld", 16)
    tq = b.chan("tq")
    vq = b.chan("vq")
    sq_ = b.chan("sq")

    NCH = 16
    x_t = b.sbuf("x", [128, NCH * ntok], F32)
    xs = [x_t[:, i * ntok:(i + 1) * ntok] for i in range(NCH)]
    zs_t = b.sbuf("zsil", [128, NCH * ntok], BF16)
    zsil = [zs_t[:, i * ntok:(i + 1) * ntok] for i in range(NCH)]
    gat_t = b.sbuf("gated", [128, NCH * ntok], BF16)
    gated = [gat_t[:, i * ntok:(i + 1) * ntok] for i in range(NCH)]

    sc = Scratch(b, ntok, nbanks=7)
    bfbank = nc.alloc_psum_tensor("bfbank", [128, 1024], BF16)
    bf_tok = [None]
    sc.register_fconst(b, 1e-5, 1e-6, 1.0)
    cready = emit_memset_consts(b, sc)
    identb = b.const("identb128", np.eye(128, dtype=BF16NP))
    ident32 = b.const("ident128f32", np.eye(128, dtype=np.float32))
    idb_sb = b.sbuf("idb", [128, 128], BF16)
    id32_sb = b.sbuf("id32", [128, 128], F32)
    ones64 = b.sbuf("ones64", [64, 128], F32)
    p.emit("vector", "memset", ones64[:, :], 1.0, inc=[vq.push()])
    ones_tok4 = vq.tok()
    nw_sb = b.sbuf("nw_sb", [128, 1], F32)
    gnw_sb = b.sbuf("gnw_sb", [128, 16], F32)
    p.dma("sync", out=idb_sb[:, :], in_=identb.ap(), inc=[ld.push()])
    p.dma("sync", out=id32_sb[:, :], in_=ident32.ap(), inc=[ld.push()])
    p.dma("sync", out=nw_sb[:, :], in_=nw, inc=[ld.push()])
    p.dma("sync", out=gnw_sb[:, :], in_=gnw, inc=[ld.push()])
    for i in range(NCH):
        p.dma("sync", out=xs[i], in_=x[i * 128:(i + 1) * 128, :], inc=[ld.push()])
    for i in range(NCH):
        p.dma("sync", out=zsil[i], in_=i_zsil[i * 128:(i + 1) * 128, :], inc=[ld.push()])
    loads = ld.tok()

    btm_sb = b.sbuf("btm_sb", [64, NCHK * 16], F32)
    gcr_t = b.sbuf("gcr", [1, NCHK * 16], F32)
    gam_col = b.sbuf("gam_col", [64, NCHK * 16], F32)
    ldb = b.chan("ldb", 16)
    for ck in range(NCHK):
        p.dma("gpsimd", out=btm_sb[:, ck * 16:(ck + 1) * 16], in_=i_btm[ck:ck + 1, :],
              inc=[ldb.push()])
        p.dma("gpsimd", out=gcr_t[0:1, ck * 16:(ck + 1) * 16],
              in_=i_btm[ck:ck + 1, 63 * 16:64 * 16], inc=[ldb.push()])
    btm_tok = ldb.tok()
    p.emit("scalar", "activation", out=gcr_t[:, :], in_=gcr_t[:, :], func=AF.Exp,
           wait=[btm_tok], inc=[sq_.push()])
    p.emit("scalar", "activation", out=gam_col[:, :], in_=btm_sb[:, :], func=AF.Exp,
           wait=[sq_.tok()], inc=[sq_.push()])
    gam_tok = sq_.tok()

    stg_w = [b.sbuf("stgw", [64, 128], F32) for _ in range(2)]
    stg_b = [b.sbuf("stgb", [128, 64], BF16) for _ in range(2)]
    stg_t = [b.sbuf("stgt", [64, 64], BF16) for _ in range(2)]
    stg_k = [b.sbuf("stgk", [64, 128], BF16) for _ in range(2)]
    stg_q = [b.sbuf("stgq", [128, 64], BF16) for _ in range(2)]
    stg_ld = [b.chan("stgld", 16), b.chan("stgld", 16)]
    stg_free = [None, None]
    ppt_sb = b.sbuf("ppt", [128, 3 * 128], BF16)
    ppt_ld = b.chan("pptld", 16)
    ppt_free = [None]
    m32 = b.sbuf("m32", [128, 128], F32)
    mbf = b.sbuf("mbf", [128, 128], BF16)
    wb16 = b.sbuf("wb16_", [64, 128], BF16)
    w32 = b.sbuf("w32_", [64, 128], F32)
    o32 = b.sbuf("o32", [128, ntok], F32)
    gcb = b.sbuf("gcb", [128, 1], F32)
    mtmp = b.sbuf("mtmp", [128, 128], F32)
    gbt = b.sbuf("gbt", [128, 64], F32)
    cb64 = b.sbuf("cb64", [64, 128], F32)
    sqs = b.sbuf("sqs", [128, ntok], F32)
    row_h = b.sbuf("row_h", [1, 2 * ntok], F32)
    ch_prev = [[]]

    def chunk_step(h, si, ck_local):
        row = ck_local * 16 + h
        cs = ck_local * 16
        t0 = ck_local * C
        buf = ck_local % 2
        ch = stg_ld[buf]
        for li, (ap_dst, ap_src) in enumerate([
                (stg_w[buf][:, :], i_wloc[row:row + 1, :]),
                (stg_b[buf][:, :], i_bfm[row:row + 1, :]),
                (stg_t[buf][:, :], i_tmt[row:row + 1, :]),
                (stg_k[buf][:, :], i_ktm[row:row + 1, :]),
                (stg_q[buf][:, :], i_qfm[row:row + 1, :])]):
            p.dma("gpsimd", out=ap_dst, in_=ap_src,
                  wait=wl(stg_free[buf]) if li == 0 else None,
                  inc=[ch.push()])
        stg_tok = ch.tok()
        prev = ch_prev[0]
        (bkg, ig) = sc.banks.get(1)[0]
        p.emit("tensor", "matmul", bkg[0:128, 0:1], sc.onesr32[:, :],
               gcr_t[0:1, cs + h:cs + h + 1], start=True, stop=True,
               wait=wl(sc.banks.wait_tok(ig),
                       *( [gam_tok, cready, ones_tok4, loads] if first_chunk[0] else [] )),
               inc=[tq.push()])
        first_chunk[0] = False
        p.emit("scalar", "copy", out=gcb[:, :], in_=bkg[0:128, 0:1], wait=[tq.tok()],
               inc=[sq_.push()])
        gcb_tok = sq_.tok()
        sc.banks.set_tok(ig, gcb_tok)
        p.emit("scalar", "mul", out=cb64[:, :], in_=ones64[:, :],
               mul=gam_col[:, cs + h:cs + h + 1],
               wait=wl(*prev, gam_tok if not prev else None, ones_tok4 if not prev else None),
               inc=[sq_.push()])
        (bkt2, it2) = sc.banks.get(1)[0]
        p.emit("tensor", "transpose", bkt2[0:128, 0:64], cb64[:, :],
               id32_sb[0:64, 0:64],
               wait=wl(sq_.tok(), sc.banks.wait_tok(it2), loads if not prev else None),
               inc=[tq.push()])
        p.emit("scalar", "copy", out=gbt[:, :], in_=bkt2[0:128, 0:64],
               wait=[tq.tok()], inc=[sq_.push()])
        gbt_tok = sq_.tok()
        sc.banks.set_tok(it2, gbt_tok)
        (bkw, iw) = sc.banks.get(1)[0]
        p.emit("tensor", "matmul", bkw[0:64, 0:128], stg_b[buf][:, :], mbf[:, :],
               start=True, stop=True,
               wait=wl(stg_tok, m_tok[0], sc.banks.wait_tok(iw)), inc=[tq.push()])
        p.emit("vector", "tensor_sub", out=w32[:, :], in0=stg_w[buf][:, :],
               in1=bkw[0:64, 0:128], wait=wl(tq.tok(), *prev), inc=[vq.push()])
        p.emit("scalar", "copy", out=wb16[:, :], in_=w32[:, :], wait=[vq.tok()],
               inc=[sq_.push()])
        w_tok = sq_.tok()
        sc.banks.set_tok(iw, w_tok)
        (bko, io) = sc.banks.get(1)[0]
        p.emit("tensor", "matmul", bko[0:128, 0:64], mbf[:, :], stg_q[buf][:, :],
               start=True, stop=True, wait=wl(m_tok[0], stg_tok, sc.banks.wait_tok(io)),
               inc=[tq.push()])
        p.emit("vector", "tensor_mul", out=o32[:, t0:t0 + C], in0=bko[0:128, 0:64],
               in1=gbt[:, :], wait=wl(tq.tok(), gbt_tok), inc=[vq.push()])
        og_tok = vq.tok()
        sc.banks.set_tok(io, og_tok)
        (bko2, io2) = sc.banks.get(1)[0]
        p.emit("tensor", "matmul", bko2[0:128, 0:64], wb16[:, :], stg_t[buf][:, :],
               start=True, stop=True, wait=wl(w_tok, sc.banks.wait_tok(io2)),
               inc=[tq.push()])
        p.emit("vector", "tensor_add", out=o32[:, t0:t0 + C], in0=o32[:, t0:t0 + C],
               in1=bko2[0:128, 0:64], wait=wl(tq.tok(), og_tok), inc=[vq.push()])
        o_tok = vq.tok()
        sc.banks.set_tok(io2, o_tok)
        (bkm, im) = sc.banks.get(1)[0]
        p.emit("tensor", "matmul", bkm[0:128, 0:128], stg_k[buf][:, :], wb16[:, :],
               start=True, stop=True, wait=wl(w_tok, sc.banks.wait_tok(im)),
               inc=[tq.push()])
        p.emit("scalar", "mul", out=mtmp[:, :], in_=m32[:, :], mul=gcb[:, 0:1],
               wait=wl(gcb_tok), inc=[sq_.push()])
        p.emit("vector", "tensor_add", out=m32[:, :], in0=mtmp[:, :],
               in1=bkm[0:128, 0:128], wait=wl(tq.tok(), sq_.tok()), inc=[vq.push()])
        p.emit("scalar", "copy", out=mbf[:, :], in_=m32[:, :], wait=[vq.tok()],
               inc=[sq_.push()])
        m_tok[0] = sq_.tok()
        sc.banks.set_tok(im, m_tok[0])
        stg_free[buf] = m_tok[0]
        ch_prev[0] = [m_tok[0], o_tok, vq.tok(), sq_.tok(), tq.tok()]

    m_tok = [None]
    first_chunk = [True]
    out_proj_wait = []
    for h in range(H):
        for si in range(2):
            s = strp[si]
            p.emit("vector", "memset", m32[:, :], 0.0, wait=wl(*ch_prev[0]),
                   inc=[vq.push()])
            p.emit("scalar", "memzero", mbf[:, :], wait=[vq.tok()],
                   inc=[sq_.push()])
            m_tok[0] = sq_.tok()
            for j in range(s):
                prow = j * 16 + h
                p.dma("gpsimd", out=ppt_sb[:, :], in_=g_pptr[prow:prow + 1, :],
                      wait=wl(ppt_free[0]), inc=[ppt_ld.push()])
                (bkf, if_) = sc.banks.get(1)[0]
                if j == 0:
                    p.emit("tensor", "matmul", bkf[0:128, 0:128], idb_sb[:, :],
                           ppt_sb[:, 256:384], start=True, stop=True,
                           wait=wl(ppt_ld.tok(), m_tok[0], sc.banks.wait_tok(if_),
                                   loads if (h == 0 and si == 0) else None),
                           inc=[tq.push()])
                else:
                    p.emit("tensor", "matmul", bkf[0:128, 0:128],
                           ppt_sb[:, 128:256], mbf[:, :], start=True, stop=False,
                           wait=wl(ppt_ld.tok(), m_tok[0], sc.banks.wait_tok(if_)))
                    p.emit("tensor", "matmul", bkf[0:128, 0:128], idb_sb[:, :],
                           ppt_sb[:, 256:384], start=False, stop=True,
                           inc=[tq.push()])
                fold_mm = tq.tok()
                p.emit("scalar", "copy", out=m32[:, :], in_=bkf[0:128, 0:128],
                       wait=[fold_mm], inc=[sq_.push()])
                p.emit("scalar", "copy", out=mbf[:, :], in_=bkf[0:128, 0:128],
                       inc=[sq_.push()])
                m_tok[0] = sq_.tok()
                ppt_free[0] = m_tok[0]
                sc.banks.set_tok(if_, m_tok[0])
            for ck in range(8):
                chunk_step(h, si, si * 8 + ck)
        p.emit("scalar", "activation", out=sqs[:, 0:ntok], in_=o32[:, 0:ntok],
               func=AF.Square, wait=wl(*ch_prev[0]), inc=[sq_.push()])
        sq_tok2 = sq_.tok()
        (bkr, ir) = sc.banks.get(1)[0]
        for hh in range(2):
            p.emit("tensor", "matmul", bkr[0:1, 0:512], sc.ones32[:, :],
                   sqs[:, hh * 512:(hh + 1) * 512], start=True, stop=True,
                   wait=wl(sq_tok2 if hh == 0 else (sq_, sq_.count),
                           sc.banks.wait_tok(ir) if hh == 0 else None),
                   inc=[tq.push()])
            p.emit("scalar", "mul", out=row_h[0:1, hh * 512:(hh + 1) * 512],
                   in_=bkr[0:1, 0:512], mul=1.0 / 128.0, wait=[tq.tok()],
                   inc=[sq_.push()])
        sc.banks.set_tok(ir, sq_.tok())
        p.emit("scalar", "activation", out=row_h[0:1, 0:ntok], in_=row_h[0:1, 0:ntok],
               func=AF.Sqrt, bias=1e-6, wait=[sq_.tok()], inc=[sq_.push()])
        p.emit("vector", "reciprocal", out=row_h[0:1, 0:ntok], in_=row_h[0:1, 0:ntok],
               wait=[sq_.tok()], inc=[vq.push()])
        rr_tok = vq.tok()
        for hh in range(2):
            (bkb2, ib2) = sc.banks.get(1)[0]
            p.emit("tensor", "matmul", bkb2[:, 0:512], sc.onesr32[:, :],
                   row_h[0:1, hh * 512:(hh + 1) * 512], start=True, stop=True,
                   wait=wl(rr_tok if hh == 0 else None, sc.banks.wait_tok(ib2)),
                   inc=[tq.push()])
            p.emit("vector", "tensor_mul", out=sqs[:, hh * 512:(hh + 1) * 512],
                   in0=o32[:, hh * 512:(hh + 1) * 512], in1=bkb2[:, 0:512],
                   wait=wl(tq.tok(), vq.tok()), inc=[vq.push()])
            sc.banks.set_tok(ib2, vq.tok())
            p.emit("scalar", "mul", out=sqs[:, hh * 512:(hh + 1) * 512],
                   in_=sqs[:, hh * 512:(hh + 1) * 512], mul=nw_sb[:, 0:1],
                   wait=[vq.tok()], inc=[sq_.push()])
            p.emit("vector", "tensor_mul", out=gated[h][:, hh * 512:(hh + 1) * 512],
                   in0=sqs[:, hh * 512:(hh + 1) * 512],
                   in1=zsil[h][:, hh * 512:(hh + 1) * 512],
                   wait=[sq_.tok()], inc=[vq.push()])
        gate_tok = vq.tok()
        ch_prev[0] = ch_prev[0] + [gate_tok, sq_.tok()]
        out_proj_wait = [gate_tok]

    def consume_outproj(oc, h0, h1, ps, mm_tok):
        p.emit("vector", "tensor_add", out=xs[oc][:, h0:h1], in0=xs[oc][:, h0:h1],
               in1=ps, wait=wl(mm_tok, vq.tok()), inc=[vq.push()])
        return vq.tok()

    emit_gemm_fm(b, sc, "op", wout, 16, list(range(16)), gated, ntok,
                 consume_outproj, wait_in=out_proj_wait)
    x3_tok = vq.tok()

    st_x3 = [b.chan("stx3", 16) for _ in range(4)]
    for i in range(NCH):
        p.dma("sync", out=x3_out[i * 128:(i + 1) * 128, :], in_=xs[i],
              wait=[x3_tok], inc=[st_x3[i % 4].push()])
    (bks, is_) = sc.banks.get(1)[0]
    prev_h = None
    for hh in range(2):
        h0, h1 = hh * 512, (hh + 1) * 512
        for i in range(NCH):
            p.emit("scalar", "activation", out=sqs[:, h0:h1], in_=xs[i][:, h0:h1],
                   func=AF.Square,
                   wait=wl(x3_tok if (i == 0 and hh == 0) else None,
                           (tq, tq.count) if i > 0 else None,
                           prev_h if i == 0 else None),
                   inc=[sq_.push()])
            p.emit("tensor", "matmul", bks[0:1, 0:512], sc.ones32[:, :], sqs[:, h0:h1],
                   start=(i == 0), stop=(i == NCH - 1),
                   wait=wl((sq_, sq_.count),
                           sc.banks.wait_tok(is_) if (i == 0 and hh == 0) else None),
                   inc=[tq.push()])
        p.emit("scalar", "mul", out=row_h[0:1, ntok + h0:ntok + h1], in_=bks[0:1, 0:512],
               mul=1.0 / D, wait=[tq.tok()], inc=[sq_.push()])
        prev_h = sq_.tok()
    p.emit("scalar", "activation", out=row_h[0:1, ntok:2 * ntok],
           in_=row_h[0:1, ntok:2 * ntok], func=AF.Sqrt, bias=1e-6, wait=[prev_h],
           inc=[sq_.push()])
    p.emit("vector", "reciprocal", out=row_h[0:1, ntok:2 * ntok],
           in_=row_h[0:1, ntok:2 * ntok], wait=[sq_.tok()], inc=[vq.push()])
    rr2_tok = vq.tok()
    sc.banks.set_tok(is_, rr2_tok)
    amax = o32
    rrb = b.sbuf("rrb", [128, ntok], F32)
    for hh in range(2):
        (bk5, i5) = sc.banks.get(1)[0]
        p.emit("tensor", "matmul", bk5[:, 0:512], sc.onesr32[:, :],
               row_h[0:1, ntok + hh * 512:ntok + (hh + 1) * 512], start=True,
               stop=True, wait=wl(rr2_tok if hh == 0 else None, sc.banks.wait_tok(i5)),
               inc=[tq.push()])
        p.emit("scalar", "copy", out=rrb[:, hh * 512:(hh + 1) * 512], in_=bk5[:, 0:512],
               wait=[tq.tok()], inc=[sq_.push()])
        sc.banks.set_tok(i5, sq_.tok())
    rrb_tok = sq_.tok()
    for i in range(NCH):
        p.emit("vector", "tensor_mul", out=sqs[:, 0:ntok], in0=xs[i], in1=rrb[:, 0:ntok],
               wait=wl(rrb_tok if i == 0 else sq_.tok(),
                       vq.tok() if i > 0 else None), inc=[vq.push()])
        p.emit("scalar", "mul", out=sqs[:, 0:ntok], in_=sqs[:, 0:ntok],
               mul=gnw_sb[:, i:i + 1], wait=[vq.tok()], inc=[sq_.push()])
        p.emit("scalar", "activation", out=(amax[:, 0:ntok] if i == 0 else sqs[:, 0:ntok]),
               in_=sqs[:, 0:ntok], func=AF.Abs, wait=[sq_.tok()], inc=[sq_.push()])
        if i > 0:
            p.emit("vector", "tensor_max", out=amax[:, 0:ntok], in0=amax[:, 0:ntok],
                   in1=sqs[:, 0:ntok], wait=wl(sq_.tok(), vq.tok()), inc=[vq.push()])
    tree_tok = (vq.tok(), sq_.tok())
    tmax = b.sbuf("tmax", [128, 8], F32)
    for j in range(ntok // 128):
        (bk6, i6) = sc.banks.get(1)[0]
        p.emit("tensor", "transpose", bk6[:, 0:128], amax[:, j * 128:(j + 1) * 128],
               id32_sb[:, :],
               wait=wl(*(tree_tok if j == 0 else ()), sc.banks.wait_tok(i6)),
               inc=[tq.push()])
        p.emit("vector", "reduce_max", tmax[0:128, j:j + 1], bk6[:, 0:128],
               axis=AX.X, wait=[tq.tok()], inc=[vq.push()])
        sc.banks.set_tok(i6, vq.tok())
    (bk7, i7) = sc.banks.get(1)[0]
    p.emit("tensor", "transpose", bk7[0:8, 0:128], tmax[:, :], id32_sb[:, :],
           wait=wl(vq.tok(), sc.banks.wait_tok(i7)), inc=[tq.push()])
    arow = b.sbuf("arow", [1, 8], F32)
    tmp8 = b.sbuf("tmp8", [8, 1], F32)
    p.emit("vector", "reduce_max", tmp8[0:8, 0:1], bk7[0:8, 0:128], axis=AX.X,
           wait=[tq.tok()], inc=[vq.push()])
    sc.banks.set_tok(i7, vq.tok())
    (bk8, i8) = sc.banks.get(1)[0]
    p.emit("tensor", "transpose", bk8[0:1, 0:8], tmp8[:, :], id32_sb[0:8, 0:8],
           wait=wl(vq.tok(), sc.banks.wait_tok(i8)), inc=[tq.push()])
    p.emit("scalar", "copy", out=arow[0:1, 0:8], in_=bk8[0:1, 0:8], wait=[tq.tok()],
           inc=[sq_.push()])
    p.emit("vector", "reduce_max", arow[0:1, 0:1], bk8[0:1, 0:8], axis=AX.X,
           wait=[sq_.tok()], inc=[vq.push()])
    sc.banks.set_tok(i8, vq.tok())
    st_ab = b.chan("stab", 16)
    p.dma("sync", out=absloc, in_=arow[0:1, 0:8], wait=[vq.tok()], inc=[st_ab.push()])
    p.barrier("sync", [st_ab.tok()] + [c.tok() for c in st_x3])
    return b.finish()



def build_launch5(ntok=1024):
    b = Builder()
    nc, p = b.nc, b.prog
    x = b.dram("x", [D, ntok], F32, kind="ExternalInput").ap()
    gam = b.dram("gam", [128, 16], F32, kind="ExternalInput").ap()
    absall = b.dram("absall", [1, 8], F32, kind="ExternalInput").ap()
    fw1 = b.dram("fw1", [D, D], F32, kind="ExternalInput").ap()
    out_dr = b.dram("out", [D, ntok], F32, kind="ExternalOutput").ap()

    ld = b.chan("ld", 16)
    tq = b.chan("tq")
    vq = b.chan("vq")
    sq_ = b.chan("sq")

    NCH = 16
    x_t = b.sbuf("x", [128, NCH * ntok], F32)
    xs = [x_t[:, i * ntok:(i + 1) * ntok] for i in range(NCH)]
    qq_t = b.sbuf("qq", [128, NCH * ntok], BF16)
    qqsb = [qq_t[:, i * ntok:(i + 1) * ntok] for i in range(NCH)]

    sc = Scratch(b, ntok, nbanks=8)
    sc.register_fconst(b, 1e-6)
    cready = emit_memset_consts(b, sc)

    for i in range(NCH):
        p.dma("sync", out=xs[i], in_=x[i * 128:(i + 1) * 128, :], inc=[ld.push()])
    gsb = b.sbuf("gsb", [128, 16], F32)
    p.dma("sync", out=gsb[:, :], in_=gam, inc=[ld.push()])
    arow = b.sbuf("arow", [1, 8], F32)
    p.dma("sync", out=arow[:, :], in_=absall, inc=[ld.push()])
    loads = ld.tok()

    row = sc.row
    sqm_row = row[0:1, 0:ntok]
    rr_row = row[0:1, ntok:2 * ntok]
    (bk_sq, iq) = sc.banks.get(1)[0]
    nhalf = ntok // 512
    prev = None
    for hh in range(nhalf):
        h0, h1 = hh * 512, (hh + 1) * 512
        for i in range(NCH):
            p.emit("scalar", "activation", out=sc.sq[:, h0:h1], in_=xs[i][:, h0:h1],
                   func=AF.Square,
                   wait=wl((loads, cready) if False else None,
                           loads if (i == 0 and hh == 0) else None,
                           cready if (i == 0 and hh == 0) else None,
                           (tq, tq.count) if i > 0 else None,
                           prev if i == 0 else None),
                   inc=[sq_.push()])
            p.emit("tensor", "matmul", bk_sq[0:1, 0:512], sc.ones32[:, :], sc.sq[:, h0:h1],
                   start=(i == 0), stop=(i == NCH - 1),
                   wait=wl((sq_, sq_.count), sc.banks.wait_tok(iq) if (i == 0 and hh == 0) else None),
                   inc=[tq.push()])
        p.emit("scalar", "mul", out=sqm_row[0:1, h0:h1], in_=bk_sq[0:1, 0:512],
               mul=1.0 / D, wait=[tq.tok()], inc=[sq_.push()])
        prev = sq_.tok()
    p.emit("scalar", "activation", out=rr_row, in_=sqm_row, func=AF.Sqrt,
           bias=1e-6, wait=[prev], inc=[sq_.push()])
    p.emit("vector", "reciprocal", out=rr_row, in_=rr_row, wait=[sq_.tok()],
           inc=[vq.push()])
    rr_tok = vq.tok()
    sc.banks.set_tok(iq, rr_tok)

    srow = row[0:1, 2 * ntok:2 * ntok + 8]
    p.emit("vector", "reduce_max", srow[0:1, 0:1], arow[:, :], axis=AX.X,
           wait=[loads], inc=[vq.push()])
    p.emit("vector", "tensor_scalar", out=srow[0:1, 0:1], in0=srow[0:1, 0:1],
           scalar1=1e-5, scalar2=None, op0=ALU.max, wait=[vq.tok()], inc=[vq.push()])
    sg_tok = vq.tok()
    p.emit("vector", "reciprocal", out=srow[0:1, 1:2], in_=srow[0:1, 0:1],
           wait=[sg_tok], inc=[vq.push()])
    p.emit("vector", "tensor_scalar", out=srow[0:1, 1:2], in0=srow[0:1, 1:2],
           scalar1=127.0, scalar2=None, op0=ALU.mult, wait=[vq.tok()], inc=[vq.push()])
    s127_tok = vq.tok()
    (bk_b, ib_) = sc.banks.get(1)[0]
    p.emit("tensor", "matmul", bk_b[:, 0:1], sc.onesr32[:, :], srow[0:1, 1:2],
           start=True, stop=True, wait=wl(s127_tok, sc.banks.wait_tok(ib_)),
           inc=[tq.push()])
    s127b = b.sbuf("s127b", [128, 1], F32)
    p.emit("scalar", "copy", out=s127b[:, :], in_=bk_b[:, 0:1], wait=[tq.tok()],
           inc=[sq_.push()])
    gs = b.sbuf("gs", [128, 16], F32)
    p.emit("scalar", "mul", out=gs[:, :], in_=gsb[:, :], mul=s127b[:, 0:1],
           wait=[sq_.tok()], inc=[sq_.push()])
    gs_tok = sq_.tok()
    sc.banks.set_tok(ib_, gs_tok)

    rrb = sc.rb
    for hh in range(nhalf):
        (bk, ir) = sc.banks.get(1)[0]
        p.emit("tensor", "matmul", bk[:, 0:512], sc.onesr32[:, :],
               rr_row[0:1, hh * 512:(hh + 1) * 512], start=True, stop=True,
               wait=wl(rr_tok if hh == 0 else None, sc.banks.wait_tok(ir)),
               inc=[tq.push()])
        p.emit("scalar", "copy", out=rrb[:, hh * 512:(hh + 1) * 512], in_=bk[:, 0:512],
               wait=[tq.tok()], inc=[sq_.push()])
        sc.banks.set_tok(ir, sq_.tok())
    rrb_tok = sq_.tok()
    tmp = sc.mb
    for i in range(NCH):
        p.emit("vector", "tensor_mul", out=tmp[:, 0:ntok], in0=xs[i], in1=rrb[:, 0:ntok],
               wait=wl(rrb_tok if i == 0 else None, gs_tok if i == 0 else None,
                       vq.tok() if i > 0 else None),
               inc=[vq.push()])
        p.emit("scalar", "mul", out=sc.sq[:, 0:ntok], in_=tmp[:, 0:ntok],
               mul=gs[:, i:i + 1], wait=[vq.tok()], inc=[sq_.push()])
        p.emit("vector", "tensor_scalar", out=qqsb[i], in0=sc.sq[:, 0:ntok],
               scalar1=MAGIC, scalar2=MAGIC, op0=ALU.add, op1=ALU.subtract,
               wait=[sq_.tok()], inc=[vq.push()])
    qq_tok = vq.tok()

    wa32 = b.sbuf("wa32", [128, 1024], F32)
    waabs = b.sbuf("waabs", [128, 1024], F32)
    wsld = [b.chan("wsld", 16), b.chan("wsld", 16)]
    (bk_w, iw) = sc.banks.get(1)[0]
    wrow = row[0:1, 3 * ntok:3 * ntok + 8]
    wfree = [None, None]
    first_mm = True
    for f in range(16):
        for qi in range(4):
            c0, c1 = qi * 512, (qi + 1) * 512
            it = f * 4 + qi
            buf = it % 2
            p.dma("gpsimd", out=wa32[:, buf * 512:(buf + 1) * 512],
                  in_=fw1[f * 128:(f + 1) * 128, c0:c1],
                  wait=wl(wfree[buf]), inc=[wsld[buf].push()])
            p.emit("scalar", "activation", out=waabs[:, buf * 512:(buf + 1) * 512],
                   in_=wa32[:, buf * 512:(buf + 1) * 512], func=AF.Abs,
                   wait=[wsld[buf].tok()], inc=[sq_.push()])
            p.emit("tensor", "matmul", bk_w[0:1, 0:512], sc.ones32[:, :],
                   waabs[:, buf * 512:(buf + 1) * 512],
                   start=first_mm, stop=(f == 15 and qi == 3),
                   wait=wl((sq_, sq_.count),
                           sc.banks.wait_tok(iw) if first_mm else None),
                   inc=[tq.push()])
            wfree[buf] = tq.tok()
            first_mm = False
    p.emit("vector", "reduce_sum", wrow[0:1, 0:1], bk_w[0:1, 0:512], axis=AX.X,
           wait=[tq.tok()], inc=[vq.push()])
    p.emit("vector", "tensor_scalar", out=wrow[0:1, 1:2], in0=wrow[0:1, 0:1],
           scalar1=1.0 / (D * D), scalar2=1e-5, op0=ALU.mult, op1=ALU.max,
           wait=[vq.tok()], inc=[vq.push()])
    sw_tok = vq.tok()
    sc.banks.set_tok(iw, sw_tok)
    p.emit("vector", "reciprocal", out=wrow[0:1, 2:3], in_=wrow[0:1, 1:2],
           wait=[sw_tok], inc=[vq.push()])
    rsw_tok = vq.tok()
    (bk_c, ic_) = sc.banks.get(1)[0]
    p.emit("tensor", "matmul", bk_c[:, 0:1], sc.onesr32[:, :], wrow[0:1, 2:3],
           start=True, stop=True, wait=wl(rsw_tok, sc.banks.wait_tok(ic_)),
           inc=[tq.push()])
    rswb = b.sbuf("rswb", [128, 1], F32)
    p.emit("scalar", "copy", out=rswb[:, :], in_=bk_c[:, 0:1], wait=[tq.tok()],
           inc=[sq_.push()])
    rswb_tok = sq_.tok()
    sc.banks.set_tok(ic_, rswb_tok)
    p.emit("vector", "tensor_mul", out=wrow[0:1, 3:4], in0=srow[0:1, 0:1],
           in1=wrow[0:1, 1:2], wait=wl(sw_tok, sg_tok), inc=[vq.push()])
    p.emit("vector", "tensor_scalar", out=wrow[0:1, 3:4], in0=wrow[0:1, 3:4],
           scalar1=1.0 / 127.0, scalar2=None, op0=ALU.mult, wait=[vq.tok()],
           inc=[vq.push()])
    deq_tok = vq.tok()
    (bk_d, id_) = sc.banks.get(1)[0]
    p.emit("tensor", "matmul", bk_d[:, 0:1], sc.onesr32[:, :], wrow[0:1, 3:4],
           start=True, stop=True, wait=wl(deq_tok, sc.banks.wait_tok(id_)),
           inc=[tq.push()])
    deqb = b.sbuf("deqb", [128, 1], F32)
    p.emit("scalar", "copy", out=deqb[:, :], in_=bk_d[:, 0:1], wait=[tq.tok()],
           inc=[sq_.push()])
    deqb_tok = sq_.tok()
    sc.banks.set_tok(id_, deqb_tok)

    wq3 = b.sbuf("wq3", [128, 512], F32)

    def wprep(src_ap, dst_ap, wld_tok):
        n = 256
        p.emit("scalar", "mul", out=wq3[:, 0:n], in_=src_ap, mul=rswb[:, 0:1],
               wait=wl(wld_tok, rswb_tok, vq.tok()), inc=[sq_.push()])
        p.emit("vector", "tensor_scalar", out=wq3[:, 0:n], in0=wq3[:, 0:n],
               scalar1=-1.0, scalar2=1.0, op0=ALU.max, op1=ALU.min,
               wait=[sq_.tok()], inc=[vq.push()])
        p.emit("vector", "tensor_scalar", out=dst_ap, in0=wq3[:, 0:n],
               scalar1=MAGIC, scalar2=MAGIC, op0=ALU.add, op1=ALU.subtract,
               wait=[vq.tok()], inc=[vq.push()])
        return vq.tok()

    st_out = [b.chan("sto", 16) for _ in range(4)]
    otmp = b.sbuf("otmp", [128, 512], F32)

    def consume(oc, h0, h1, ps, mm_tok):
        p.emit("scalar", "mul", out=otmp[:, 0:h1 - h0], in_=ps, mul=deqb[:, 0:1],
               wait=wl(mm_tok, deqb_tok, vq.tok()), inc=[sq_.push()])
        ps_done = sq_.tok()
        p.emit("vector", "tensor_add", out=xs[oc][:, h0:h1], in0=xs[oc][:, h0:h1],
               in1=otmp[:, 0:h1 - h0], wait=[ps_done], inc=[vq.push()])
        p.dma("sync", out=out_dr[oc * 128:(oc + 1) * 128, h0:h1], in_=xs[oc][:, h0:h1],
              wait=[vq.tok()], inc=[st_out[oc % 4].push()])
        return ps_done

    emit_gemm_fm(b, sc, "f1", fw1, 16, list(range(16)), qqsb, ntok,
                 consume, wait_in=[qq_tok], wprep=wprep)
    p.barrier("sync", [c.tok() for c in st_out])
    return b.finish()



def _toks(core):
    s0, s1 = stripes_of(core)
    return np.r_[s0 * 512:(s0 + 1) * 512, s1 * 512:(s1 + 1) * 512]


def _pack_pp(w):
    return np.ascontiguousarray(w.reshape(16, 128).T)


def kernel(L0_attn_x, L0_attn_w_qkv, L0_attn_w_out, L0_ffn_weight, L0_ffn_w_scale,
           L0_ffn_bias, L1_attn_Wqkv, L1_attn_Wz, L1_attn_Wb, L1_attn_Wa,
           L1_attn_conv_w, L1_attn_A_log, L1_attn_dt_bias, L1_attn_norm_w,
           L1_attn_Wout, L1_ffn_weight, L1_ffn_norm_weight):
    x0 = np.ascontiguousarray(np.asarray(L0_attn_x, dtype=np.float32)[0])
    wqkv0 = np.ascontiguousarray(np.asarray(L0_attn_w_qkv, dtype=np.float32))
    wout0 = np.ascontiguousarray(np.asarray(L0_attn_w_out, dtype=np.float32))
    fw0T = np.ascontiguousarray(np.asarray(L0_ffn_weight).T)
    fsc0 = _pack_pp(np.asarray(L0_ffn_w_scale, dtype=np.float32))
    fb0 = _pack_pp(np.asarray(L0_ffn_bias, dtype=np.float32))
    wqkv1 = np.ascontiguousarray(np.asarray(L1_attn_Wqkv, dtype=np.float32))
    wz1 = np.ascontiguousarray(np.asarray(L1_attn_Wz, dtype=np.float32))
    wb1 = np.asarray(L1_attn_Wb, dtype=np.float32)
    wa1 = np.asarray(L1_attn_Wa, dtype=np.float32)
    cw1 = np.asarray(L1_attn_conv_w, dtype=np.float32).reshape(3 * D, 4)
    alog1 = np.ascontiguousarray(np.asarray(L1_attn_A_log, dtype=np.float32).reshape(16, 1))
    dtb1 = np.ascontiguousarray(np.asarray(L1_attn_dt_bias, dtype=np.float32).reshape(16, 1))
    nw1 = np.ascontiguousarray(np.asarray(L1_attn_norm_w, dtype=np.float32).reshape(128, 1))
    wout1 = np.ascontiguousarray(np.asarray(L1_attn_Wout, dtype=np.float32))
    fw1T = np.ascontiguousarray(np.asarray(L1_ffn_weight, dtype=np.float32).T)
    gnw1 = _pack_pp(np.asarray(L1_ffn_norm_weight, dtype=np.float32))
    wa_p = np.ascontiguousarray(wa1.reshape(16, 128, 16).transpose(1, 0, 2).reshape(128, 256))
    wb_p = np.ascontiguousarray(wb1.reshape(16, 128, 16).transpose(1, 0, 2).reshape(128, 256))
    cw_p = np.ascontiguousarray(cw1.reshape(48, 128, 4).transpose(1, 0, 2).reshape(128, 192))

    cores = list(range(NCORES))

    nc1 = build_launch1(1024)
    ims = [{"x": np.ascontiguousarray(x0[_toks(c)].T), "wqkv": wqkv0} for c in cores]
    r1 = run_bass_kernel_spmd(nc1, ims, core_ids=cores).results

    Kf = np.zeros((D, S), dtype=BF16NP)
    Vf = np.zeros((S, D), dtype=BF16NP)
    knorm = np.zeros((1, 256), dtype=np.float32)
    qn = {}
    for c in cores:
        s0, s1 = stripes_of(c)
        Ko = np.asarray(r1[c]["K_out"])
        Qo = np.asarray(r1[c]["Q_out"])
        Vo = np.asarray(r1[c]["V_out"])
        nr = np.asarray(r1[c]["norms_out"])[0]
        Kf[:, s0 * 512:(s0 + 1) * 512] = Ko[:, 0:512]
        Kf[:, s1 * 512:(s1 + 1) * 512] = Ko[:, 512:1024]
        Vf[s0 * 512:(s0 + 1) * 512] = Vo[0:512]
        Vf[s1 * 512:(s1 + 1) * 512] = Vo[512:1024]
        qn[c] = (np.asarray(r1[c]["Q_out"]), nr)
        for h in range(16):
            for si, s in enumerate([s0, s1]):
                knorm[0, h * 16 + s] = nr[32 + h * 2 + si]

    x2 = {}
    for c in cores:
        s0, s1 = stripes_of(c)
        nr = qn[c][1]
        qnorm = np.zeros((1, 32), dtype=np.float32)
        for h in range(16):
            for si in range(2):
                qnorm[0, h * 2 + si] = nr[h * 2 + si]
        im = {"x": np.ascontiguousarray(x0[_toks(c)].T), "Qo": qn[c][0],
              "Kf": Kf, "Vf": Vf, "qnorm": qnorm, "knorm": knorm,
              "wout": wout0, "fw": fw0T, "fsc": fsc0, "fbias": fb0}
        nc2 = build_launch2(c)
        x2[c] = np.asarray(run_bass_kernel_spmd(nc2, [im], core_ids=[c]).results[0]["x2_out"])

    x2f = np.zeros((S, D), dtype=np.float32)
    for c in cores:
        x2f[_toks(c)] = x2[c].T

    nc3 = build_launch3()
    ims = []
    for c in cores:
        s0, s1 = stripes_of(c)
        xc = np.zeros((D, 1032), dtype=np.float32)
        xc[:, 0:1024] = x2[c]
        for si, s in enumerate([s0, s1]):
            if s > 0:
                xc[:, 1024 + si * 3:1024 + si * 3 + 3] = x2f[s * 512 - 3:s * 512].T
        ims.append({"x": xc, "wqkv": wqkv1, "wz": wz1, "wa": wa_p, "wb": wb_p,
                    "convw": cw_p, "alog": alog1, "dtb": dtb1})
    r3 = run_bass_kernel_spmd(nc3, ims, core_ids=cores).results

    g_pptr = np.zeros((NSTRIPE * 16, 3 * 128 * 128), dtype=BF16NP)
    for c in cores:
        s0, s1 = stripes_of(c)
        pp = np.asarray(r3[c]["o_pptr"])
        for si, s in enumerate([s0, s1]):
            g_pptr[s * 16:(s + 1) * 16] = pp[si * 16:(si + 1) * 16]

    x3 = {}
    absall = np.zeros((1, 8), dtype=np.float32)
    for c in cores:
        im = {"x": x2[c], "g_pptr": g_pptr,
              "i_wloc": np.asarray(r3[c]["o_wloc"]), "i_bfm": np.asarray(r3[c]["o_bfm"]),
              "i_tmt": np.asarray(r3[c]["o_tmt"]), "i_ktm": np.asarray(r3[c]["o_ktm"]),
              "i_qfm": np.asarray(r3[c]["o_qfm"]), "i_btm": np.asarray(r3[c]["o_btm"]),
              "i_zsil": np.asarray(r3[c]["o_zsil"]),
              "wout": wout1, "nw": nw1, "gnw": gnw1}
        nc4 = build_launch4(c)
        r4 = run_bass_kernel_spmd(nc4, [im], core_ids=[c]).results[0]
        x3[c] = np.asarray(r4["x3_out"])
        absall[0, c] = np.asarray(r4["absloc"])[0, 0]

    nc5 = build_launch5(1024)
    ims = [{"x": x3[c], "gam": gnw1, "absall": absall, "fw1": fw1T} for c in cores]
    r5 = run_bass_kernel_spmd(nc5, ims, core_ids=cores).results

    out = np.zeros((S, D), dtype=np.float32)
    for c in cores:
        out[_toks(c)] = np.asarray(r5[c]["out"]).T
    return out.reshape(1, S, D)
